# version 05 (final)
# baseline (speedup 1.0000x reference)
import os as _os_env
_os_env.environ.setdefault("NEURON_SCRATCHPAD_PAGE_SIZE", "768")
import numpy as np
import ml_dtypes
from contextlib import ExitStack
import concourse.bass as bass
import concourse.mybir as mybir

F32 = mybir.dt.float32
BF16 = mybir.dt.bfloat16
U32 = mybir.dt.uint32
U8 = mybir.dt.uint8
AF = mybir.ActivationFunctionType
ALU = mybir.AluOpType
AX = mybir.AxisListType
IOA = bass.IndirectOffsetOnAxis

E = 1024
H = 16
D = 64
C = 128
B = 2
T = 4096
NCH = B * T // C
NCORE = 8
NTOK = 65536
CAP = 9831
CAPP = 9856
MAPN = 78976
MAPC = 617
SCALE = D ** -0.5
BIG = 1024.0
LN_EPS = 1e-5
RMS_EPS = 1e-5
RT_EPS = 1e-9


class Sem:
    def __init__(self, ctx, nc, name):
        self.h = ctx.enter_context(nc.semaphore(name))
        self.n = 0


class P:
    ENGS = ("sync", "vector", "scalar", "tensor", "gpsimd")
    AUTOEDGE = ("vector", "scalar", "gpsimd")
    DMA_METHODS = ("dma_start", "indirect_dma_start", "collective_compute",
                   "dma_start_transpose")

    def __init__(self, nc, ctx):
        self.nc = nc
        self.ctx = ctx
        self.ops = {k: [] for k in self.ENGS}
        self.sems = []
        self.auto = {e: Sem(ctx, nc, f"auto_{e}") for e in self.AUTOEDGE}
        self.last_edge = {e: None for e in self.AUTOEDGE}

    def sem(self, name):
        if not hasattr(self, "_semnames"):
            self._semnames = {}
        k = self._semnames.get(name, 0)
        self._semnames[name] = k + 1
        s = Sem(self.ctx, self.nc, name if k == 0 else f"{name}_{k}")
        self.sems.append(s)
        return s

    def drain_all(self):
        for s in self.sems:
            if s.n > 0:
                self.wait("gpsimd", s, s.n)

    def wait(self, eng, sem, val):
        if val > 0:
            self.ops[eng].append(("wait", sem.h, val))

    def do(self, eng, method, *args, inc=None, inck=1, **kwargs):
        tgt = None
        auto = None
        edged = eng in self.AUTOEDGE and method not in self.DMA_METHODS
        if edged and self.last_edge[eng] is not None:
            self.ops[eng].append(("wait",) + self.last_edge[eng])
        if inc is not None:
            inc.n += inck
            tgt = inc.n
            if edged:
                self.last_edge[eng] = (inc.h, tgt)
        elif edged:
            a = self.auto[eng]
            a.n += 1
            auto = (a.h, 1)
            self.last_edge[eng] = (a.h, a.n)
        self.ops[eng].append(("op", method, args, kwargs,
                              (inc.h if inc is not None else None), inck, auto))
        return tgt

    def play(self, block):
        def runner(ops):
            def run(e):
                for item in ops:
                    if item[0] == "wait":
                        e.wait_ge(item[1], item[2])
                    else:
                        _, method, args, kwargs, semh, k, auto = item
                        try:
                            inst = getattr(e, method)(*args, **kwargs)
                        except Exception as ex:
                            raise RuntimeError(
                                f"op failed: {method} kwargs={list(kwargs)}") from ex
                        if auto is not None:
                            inst.then_inc(auto[0], auto[1])
                        if semh is not None:
                            inst.then_inc(semh, k)
            return run
        r = {k: runner(v) for k, v in self.ops.items()}
        block.sync(r["sync"]); block.vector(r["vector"]); block.scalar(r["scalar"])
        block.tensor(r["tensor"]); block.gpsimd(r["gpsimd"])


def host_consts():
    i = np.arange(128)
    return {
        "eye_f": np.eye(128, dtype=np.float32),
        "eye_b": np.eye(128).astype(ml_dtypes.bfloat16),
        "u128f": np.triu(np.ones((128, 128), np.float32)),
        "u128b": np.triu(np.ones((128, 128))).astype(ml_dtypes.bfloat16),
        "u128sf": np.triu(np.ones((128, 128), np.float32), 1),
        "u8sf": np.triu(np.ones((8, 8), np.float32), 1),
        "mask_st": (i[:, None] <= i[None, :]).astype(np.uint8),
        "iota8": np.tile(np.arange(8, dtype=np.float32), (128, 1)),
        "iotacol": i.astype(np.float32)[:, None],
    }


CONST_DTYPES = {"eye_f": F32, "eye_b": BF16, "u128f": F32, "u128b": BF16,
                "u128sf": F32, "u8sf": F32, "mask_st": U8, "iota8": F32,
                "iotacol": F32}

KVALS = [1e-5, -0.5, -1.0, float(np.log(1.0 / 8.0)), 1.0 / 64.0, 448.0, 76.0, 63.5]


def register_kvals(p, sb, s_w):
    nc = p.nc
    arr = np.tile(np.array(KVALS, np.float32), (128, 1))
    kd = nc.inline_tensor(arr, name="kvals_d")
    konst = sb("konst", [128, len(KVALS)], F32)
    p.do("sync", "dma_start", out=konst[:], in_=kd.ap(), inc=s_w, inck=16)
    for i, v in enumerate(KVALS):
        nc.const_aps.register(F32, float(v), konst[:, i:i + 1])
    return konst


def build_attn(p, io, sb, ps):
    nc = p.nc
    s_x = p.sem("s_x"); s_w = p.sem("s_w"); s_v = p.sem("s_v")
    s_a = p.sem("s_a"); s_t = p.sem("s_t"); s_st = p.sem("s_st"); s_g = p.sem("s_g")

    cn = {name: sb("c_" + name, list(host_consts()[name].shape), CONST_DTYPES[name])
          for name in CONST_DTYPES}
    ones1f = sb("ones1f", [1, 128], F32)
    onescol = sb("onescol", [128, 1], F32)
    coreinfo = sb("cinfo_s", [128, 8], F32)

    for name in cn:
        p.do("sync", "dma_start", out=cn[name][:], in_=io[name], inc=s_w, inck=16)
    p.do("sync", "dma_start", out=coreinfo[:], in_=io["coreinfo"], inc=s_w, inck=16)
    register_kvals(p, sb, s_w)

    wq_b = sb("wq_b", [128, 8 * 1024], F32)
    wk_b = sb("wk_b", [128, 8 * 1024], F32)
    wst = sb("wst", [128, 8 * 1024], F32)
    wv_b = wst
    wo_b = wst
    wf1_b = sb("wf1_b", [128, 8 * 64], F32)
    wg1_b = sb("wg1_b", [128, 8 * 64], F32)
    wf2_b = sb("wf2_b", [64, 1024], F32)
    wg2_b = sb("wg2_b", [64, 1024], F32)
    rwT = sb("rwT", [128, 8 * 8], F32)
    rbrow = sb("rbrow", [1, 8], F32)
    wo_stg = sb("wo_stg", [128, 1024], F32)
    nw_c = sb("nw_c", [128, 8], F32)
    nw_r = sb("nw_r", [8, 128], F32)
    rw_s = sb("rw_s", [8, 1024], F32)

    for wname, dst, nblk in [("wq", wq_b, 1024), ("wk", wk_b, 1024),
                             ("wf1", wf1_b, 64), ("wg1", wg1_b, 64)]:
        p.do("gpsimd", "dma_start",
             out=dst[:].rearrange("p (k n) -> p k n", n=nblk),
             in_=io[wname].rearrange("(k p) n -> p k n", p=128), inc=s_g, inck=16)
    p.do("gpsimd", "dma_start", out=wf2_b[:], in_=io["wf2"], inc=s_g, inck=16)
    p.do("gpsimd", "dma_start", out=wg2_b[:], in_=io["wg2"], inc=s_g, inck=16)
    G_WCAST = s_g.n
    p.do("sync", "dma_start", out=nw_r[:], in_=io["nw"].rearrange("(k n) -> k n", n=128),
         inc=s_w, inck=16)
    p.do("sync", "dma_start", out=rw_s[:], in_=io["rw"], inc=s_w, inck=16)
    p.do("sync", "dma_start", out=rbrow[:], in_=io["rb"].rearrange("(a e) -> a e", a=1),
         inc=s_w, inck=16)
    W_ALL = s_w.n

    pq = [ps("pq0", [128, 512], F32), ps("pq1", [128, 512], F32)]
    pk = [ps("pk0", [128, 512], F32), ps("pk1", [128, 512], F32)]
    pt = [ps("pt0", [128, 512], F32), ps("pt1", [128, 512], F32)]
    pb = [ps("pb0", [128, 512], F32), ps("pb1", [128, 512], F32)]

    p.wait("tensor", s_w, W_ALL)
    p.do("tensor", "transpose", pt[0][:, 0:8], nw_r[:], cn["eye_f"][0:8, 0:8], inc=s_t)
    for k in range(8):
        p.do("tensor", "transpose", pt[1][:, 8 * k:8 * k + 8],
             rw_s[:, 128 * k:128 * (k + 1)], cn["eye_f"][0:8, 0:8],
             inc=s_t if k == 7 else None)
    T_PREP = s_t.n
    p.wait("vector", s_t, T_PREP)
    p.do("vector", "tensor_copy", out=nw_c[:], in_=pt[0][:, 0:8])
    p.do("vector", "tensor_copy", out=rwT[:], in_=pt[1][:, 0:64], inc=s_v)
    s_wo = p.sem("s_wo")
    s_wos = p.sem("s_wos")
    wo_r = io["wo"].rearrange("(k p) n -> k p n", p=128)
    wod_r = io["wod_d"].rearrange("(k p) n -> k p n", p=128)
    for k in range(8):
        if k >= 1:
            p.wait("sync", s_wos, s_wos.n)
        p.do("sync", "dma_start", out=wo_stg[:], in_=wo_r[k], inc=s_wo, inck=16)
        p.wait("vector", s_wo, s_wo.n)
        WOS = p.do("vector", "tensor_scalar_mul",
                   out=wo_stg[:], in0=wo_stg[:],
                   scalar1=nw_c[:, k:k + 1], inc=s_v)
        p.wait("scalar", s_v, WOS)
        p.do("scalar", "dma_start", out=wod_r[k], in_=wo_stg[:], inc=s_wos, inck=16)
    p.do("vector", "memset", ones1f[:], 1.0)
    p.do("vector", "memset", onescol[:], 1.0, inc=s_v)
    V_WPREP = s_v.n

    def sb2(name, shp, dt=F32):
        return [sb(name + "0", shp, dt), sb(name + "1", shp, dt)]
    def sb1(name, shp, dt=F32):
        t = sb(name, shp, dt)
        return [t, t]
    x_sb = sb2("x_sb", [128, E]); xnb = sb1("xnb", [128, E], F32)
    xn_fm = sb1("xn_fm", [128, E], F32)
    q_s = sb1("q_s", [128, E]); k_s = sb1("k_s", [128, E]); v_b = sb1("v_b", [128, E], F32)
    gg = sb1("gg", [128, E])
    zf1b = sb1("zf1b", [64, 128], F32); zg1b = sb1("zg1b", [64, 128], F32)
    u_t = sb1("u_t", [128, E]); vt_t = sb1("vt_t", [128, E])
    vc_s = sb1("vc_s", [128, E])
    dm_t = vt_t
    e_t = sb1("e_t", [128, E])
    ei_t = u_t
    qt_tm = q_s; kt_tm = k_s
    qt_fm = sb1("qt_fm", [128, E], F32); kt_fm = sb1("kt_fm", [128, E], F32)
    og_t = sb1("og_t", [128, E]); ogb = og_t
    og_fm = xn_fm
    at_sb = sb1("at_sb", [128, E]); xn2_t = at_sb
    xn2fm = qt_fm
    am_sb = sb2("am_sb", [128, 128], F32)
    st_ = sb2("st_", [128, 24])
    vmc = sb2("vmc", [128, 8]); vcc = sb2("vcc", [128, 8])
    vm_row = sb("vm_row", [1, E]); vc_row = sb("vc_row", [1, E])
    ebm = sb2("ebm", [128, 8]); ebcm = sb2("ebcm", [128, 8]); ebc = sb2("ebc", [128, 8])
    sm_t = sb2("sm_t", [128, 8]); top8 = sb2("top8", [128, 8])
    Ss_t = [sb1(f"Ss_{pr}", [128, D]) for pr in range(8)]
    _sn = sb1("Sn_sh", [128, D])
    Sn_t = [_sn for _ in range(8)]
    S_t = [[sb(f"S_{b}_{pr}", [128, D]) for pr in range(8)] for b in range(2)]
    sink = sb("sink", [128, E])
    e0sb = sb("e0sb", [128, 64]); e1sb = sb("e1sb", [128, 64])
    w0sb = sb("w0sb", [128, 64]); w1sb = sb("w1sb", [128, 64])

    for arr in (e0sb, e1sb, w0sb, w1sb):
        p.do("vector", "memset", arr[:], 0.0)
    for b in range(2):
        for pr in range(8):
            p.do("vector", "memset", S_t[b][pr][:], 0.0,
                 inc=s_v if (b == 1 and pr == 7) else None)
    V_S0 = s_v.n

    p.wait("tensor", s_g, G_WCAST)
    p.wait("tensor", s_v, V_WPREP)
    p.wait("scalar", s_v, V_S0)

    s_wv = p.sem("s_wv")
    TWOm = [0] * NCH
    x_r = io["x"]
    VCH = [0] * NCH
    AHEAD = [0] * NCH
    VHEADS = [0] * NCH
    STM = [0] * NCH
    GXN = [0] * NCH
    tmarks = {"pb": [0, 0], "pt": [0, 0]}

    def trans_evac(src_slices, dst_slices, eye, which):
        banks = pb if which == "pb" else pt
        marks = tmarks[which]
        for k, (srcap, dstap) in enumerate(zip(src_slices, dst_slices)):
            tb = k % 2
            if marks[tb]:
                p.wait("tensor", s_v, marks[tb])
            tgt_t = p.do("tensor", "transpose", banks[tb][:, 0:128], srcap, eye, inc=s_t)
            p.wait("vector", s_t, tgt_t)
            marks[tb] = p.do("vector", "tensor_copy", out=dstap,
                             in_=banks[tb][:, 0:128], inc=s_v)
        return s_v.n

    for j in range(NCH):
        b = j % 2
        ci = j // 2
        par = j % 2
        lcol = 32 * b + ci
        X = x_sb[par]; XNB = xnb[par]; XNFM = xn_fm[par]
        QS = q_s[par]; KS = k_s[par]; VB = v_b[par]; GG = gg[par]
        ZF1 = zf1b[par]; ZG1 = zg1b[par]
        UT = u_t[par]; VT = vt_t[par]; VC = vc_s[par]; DM = dm_t[par]
        ET = e_t[par]; EIT = ei_t[par]
        QTT = qt_tm[par]; KTT = kt_tm[par]; QTF = qt_fm[par]; KTF = kt_fm[par]
        OG = og_t[par]; OGB = ogb[par]; OGF = og_fm[par]
        AT = at_sb[par]; XN2 = xn2_t[par]; XN2F = xn2fm[par]
        ST = st_[par]; VMC = vmc[par]; VCC = vcc[par]
        EBM = ebm[par]; EBCM = ebcm[par]; EBCV = ebc[par]; SM = sm_t[par]; T8 = top8[par]

        if j >= 2:
            p.wait("sync", s_v, VCH[j - 2])
        p.do("sync", "dma_start", out=X[:], in_=x_r[b, 128 * ci:128 * (ci + 1), :],
             inc=s_x, inck=16)
        X_j = s_x.n

        p.wait("scalar", s_x, X_j)
        p.do("scalar", "activation", out=sink[:], in_=X[:], func=AF.Identity,
             accum_out=ST[:, 0:1])
        A_ST1 = p.do("scalar", "activation", out=sink[:], in_=X[:], func=AF.Square,
                     accum_out=ST[:, 1:2], inc=s_a)
        p.wait("vector", s_a, A_ST1)
        p.do("vector", "tensor_scalar_mul", out=ST[:, 2:3], in0=ST[:, 0:1], scalar1=1.0 / E)
        p.do("vector", "tensor_scalar", out=ST[:, 3:4], in0=ST[:, 2:3],
             scalar1=ST[:, 2:3], scalar2=None, op0=ALU.mult)
        V_VAR = p.do("vector", "tensor_scalar", out=ST[:, 4:5], in0=ST[:, 1:2],
                     scalar1=1.0 / E, scalar2=ST[:, 3:4], op0=ALU.mult, op1=ALU.subtract,
                     inc=s_v)
        p.wait("scalar", s_v, V_VAR)
        p.do("scalar", "activation", out=ST[:, 5:6], in_=ST[:, 4:5], func=AF.Ln, bias=LN_EPS)
        A_RSTD = p.do("scalar", "activation", out=ST[:, 6:7], in_=ST[:, 5:6],
                      func=AF.Exp, scale=-0.5, inc=s_a)
        p.wait("vector", s_a, A_RSTD)
        V_XN = p.do("vector", "tensor_scalar", out=XNB[:], in0=X[:],
                    scalar1=ST[:, 2:3], scalar2=ST[:, 6:7],
                    op0=ALU.subtract, op1=ALU.mult, inc=s_v)

        p.wait("tensor", s_v, V_XN)
        V_XNFM = trans_evac([XNB[:, 128 * k:128 * (k + 1)] for k in range(8)],
                            [XNFM[:, 128 * k:128 * (k + 1)] for k in range(8)],
                            cn["eye_f"][:], "pb")

        p.wait("tensor", s_v, V_XNFM)
        if j >= 1:
            p.wait("tensor", s_a, AHEAD[j - 1])
            p.wait("tensor", s_v, VHEADS[j - 1])
        for half in range(2):
            for k in range(8):
                p.do("tensor", "matmul", pq[half][:], XNFM[:, 128 * k:128 * (k + 1)],
                     wq_b[:, 1024 * k + 512 * half:1024 * k + 512 * (half + 1)],
                     start=(k == 0), stop=(k == 7),
                     inc=s_t if (half == 1 and k == 7) else None)
        T_Q = s_t.n
        p.wait("scalar", s_t, T_Q)
        p.do("scalar", "activation", out=QS[:, 0:512], in_=pq[0][:], func=AF.Silu)
        A_SILUQ = p.do("scalar", "activation", out=QS[:, 512:1024], in_=pq[1][:],
                       func=AF.Silu, inc=s_a)
        for half in range(2):
            for k in range(8):
                p.do("tensor", "matmul", pk[half][:], XNFM[:, 128 * k:128 * (k + 1)],
                     wk_b[:, 1024 * k + 512 * half:1024 * k + 512 * (half + 1)],
                     start=(k == 0), stop=(k == 7),
                     inc=s_t if (half == 1 and k == 7) else None)
        T_K = s_t.n
        p.wait("scalar", s_t, T_K)
        p.do("scalar", "activation", out=KS[:, 0:512], in_=pk[0][:], func=AF.Silu)
        A_SILUK = p.do("scalar", "activation", out=KS[:, 512:1024], in_=pk[1][:],
                       func=AF.Silu, inc=s_a)
        if j >= 1:
            p.wait("sync", s_t, TWOm[j - 1])
        p.do("sync", "dma_start", out=wst[:].rearrange("p (k n) -> p k n", n=1024),
             in_=io["wv"].rearrange("(k p) n -> p k n", p=128), inc=s_wv, inck=16)
        WV_L = s_wv.n
        p.wait("tensor", s_a, A_SILUQ)
        p.wait("tensor", s_wv, WV_L)
        for half in range(2):
            for k in range(8):
                p.do("tensor", "matmul", pq[half][:], XNFM[:, 128 * k:128 * (k + 1)],
                     wv_b[:, 1024 * k + 512 * half:1024 * k + 512 * (half + 1)],
                     start=(k == 0), stop=(k == 7),
                     inc=s_t if (half == 1 and k == 7) else None)
        T_V = s_t.n
        p.wait("vector", s_t, T_V)
        p.do("vector", "tensor_copy", out=VB[:, 0:512], in_=pq[0][:])
        V_VB = p.do("vector", "tensor_copy", out=VB[:, 512:1024], in_=pq[1][:], inc=s_v)
        p.wait("tensor", s_a, A_SILUK)
        for k in range(8):
            p.do("tensor", "matmul", pk[0][0:64, 0:128], wf1_b[:, 64 * k:64 * (k + 1)],
                 XNFM[:, 128 * k:128 * (k + 1)], start=(k == 0), stop=(k == 7))
        for k in range(8):
            p.do("tensor", "matmul", pk[1][0:64, 0:128], wg1_b[:, 64 * k:64 * (k + 1)],
                 XNFM[:, 128 * k:128 * (k + 1)], start=(k == 0), stop=(k == 7),
                 inc=s_t if k == 7 else None)
        T_Z1 = s_t.n
        p.wait("vector", s_t, T_Z1)
        p.do("vector", "tensor_copy", out=ZF1[:], in_=pk[0][0:64, 0:128])
        V_Z1 = p.do("vector", "tensor_copy", out=ZG1[:], in_=pk[1][0:64, 0:128], inc=s_v)
        p.wait("tensor", s_v, max(V_VB, V_Z1))
        for half in range(2):
            p.do("tensor", "matmul", pq[half][:], ZF1[:], wf2_b[:, 512 * half:512 * (half + 1)],
                 start=True, stop=True, inc=s_t if half == 1 else None)
        T_ZF = s_t.n
        for half in range(2):
            p.do("tensor", "matmul", pk[half][:], ZG1[:], wg2_b[:, 512 * half:512 * (half + 1)],
                 start=True, stop=True, inc=s_t if half == 1 else None)
        T_ZG = s_t.n
        p.wait("scalar", s_t, T_ZF)
        p.do("scalar", "activation", out=UT[:, 0:512], in_=pq[0][:], func=AF.Exp, scale=-1.0)
        p.do("scalar", "activation", out=UT[:, 512:1024], in_=pq[1][:], func=AF.Exp, scale=-1.0)
        A_VT = p.do("scalar", "activation", out=VT[:], in_=UT[:], func=AF.Ln, bias=1.0,
                    inc=s_a)
        p.wait("scalar", s_t, T_ZG)
        p.do("scalar", "activation", out=UT[:, 0:512], in_=pk[0][:], func=AF.Exp, scale=-1.0)
        A_UG = p.do("scalar", "activation", out=UT[:, 512:1024], in_=pk[1][:],
                    func=AF.Exp, scale=-1.0, inc=s_a)
        p.wait("vector", s_a, A_UG)
        p.do("vector", "tensor_scalar_add", out=GG[:], in0=UT[:], scalar1=1.0)
        V_GG = p.do("vector", "reciprocal", out=GG[:], in_=GG[:], inc=s_v)
        p.wait("tensor", s_a, A_VT)
        for half in range(2):
            p.do("tensor", "matmul", pq[half][:], cn["u128f"][:],
                 VT[:, 512 * half:512 * (half + 1)], start=True, stop=True,
                 inc=s_t if half == 1 else None)
        T_B = s_t.n
        p.wait("scalar", s_t, T_B)
        p.do("scalar", "activation", out=VC[:, 0:512], in_=pq[0][:], func=AF.Identity)
        A_VC = p.do("scalar", "activation", out=VC[:, 512:1024], in_=pq[1][:],
                    func=AF.Identity, inc=s_a)
        p.wait("tensor", s_a, max(A_VT, A_UG))
        vmb_banks = ((pt[0], pk[0]), (pt[1], pk[1]))
        for half in range(2):
            p.do("tensor", "matmul", vmb_banks[0][half][0:1, 0:512],
                 onescol[0:64, :],
                 VT[0:64, 512 * half:512 * (half + 1)], start=True, stop=True)
            p.do("tensor", "matmul", vmb_banks[1][half][0:1, 0:512],
                 onescol[:],
                 VT[:, 512 * half:512 * (half + 1)], start=True, stop=True,
                 inc=s_t if half == 1 else None)
        T_VR = s_t.n
        p.wait("vector", s_t, T_VR)
        p.do("vector", "tensor_copy", out=vm_row[0:1, 0:512], in_=pt[0][0:1, 0:512])
        p.do("vector", "tensor_copy", out=vm_row[0:1, 512:1024], in_=pk[0][0:1, 0:512])
        p.do("vector", "tensor_copy", out=vc_row[0:1, 0:512], in_=pt[1][0:1, 0:512])
        V_VR = p.do("vector", "tensor_copy", out=vc_row[0:1, 512:1024], in_=pk[1][0:1, 0:512],
                    inc=s_v)
        p.wait("tensor", s_v, V_VR)
        p.wait("tensor", s_a, A_VC)
        for half in range(2):
            p.do("tensor", "matmul", pk[half][:], ones1f[:],
                 vm_row[0:1, 512 * half:512 * (half + 1)], start=True, stop=True,
                 inc=s_t if half == 1 else None)
        T_VMB = s_t.n
        p.wait("vector", s_t, T_VMB)
        p.do("vector", "tensor_tensor", out=DM[:, 0:512], in0=VC[:, 0:512], in1=pk[0][:],
             op=ALU.subtract)
        V_DM = p.do("vector", "tensor_tensor", out=DM[:, 512:1024], in0=VC[:, 512:1024],
                    in1=pk[1][:], op=ALU.subtract, inc=s_v)
        p.wait("scalar", s_v, V_DM)
        p.do("scalar", "activation", out=ET[:], in_=DM[:], func=AF.Exp, scale=-1.0,
             bias=float(np.log(SCALE)))
        A_E = p.do("scalar", "activation", out=EIT[:], in_=DM[:], func=AF.Exp, inc=s_a)
        p.wait("vector", s_a, A_E)
        p.do("vector", "tensor_tensor", out=QTT[:], in0=QS[:], in1=ET[:], op=ALU.mult)
        V_QK = p.do("vector", "tensor_tensor", out=KTT[:], in0=KS[:], in1=EIT[:],
                    op=ALU.mult, inc=s_v)
        p.wait("tensor", s_v, V_QK)
        trans_evac([QTT[:, 128 * k:128 * (k + 1)] for k in range(8)],
                   [QTF[:, 128 * k:128 * (k + 1)] for k in range(8)], cn["eye_f"][:], "pb")
        V_FM = trans_evac([KTT[:, 128 * k:128 * (k + 1)] for k in range(8)],
                          [KTF[:, 128 * k:128 * (k + 1)] for k in range(8)],
                          cn["eye_f"][:], "pb")
        for pr in range(8):
            for which, srow, dstcol in ((0, vm_row, VMC), (1, vc_row, VCC)):
                if tmarks["pt"][which]:
                    p.wait("tensor", s_v, tmarks["pt"][which])
                tt = p.do("tensor", "transpose", pt[which][:, 0:1],
                          srow[0:1, 128 * pr:128 * (pr + 1)],
                          cn["eye_f"][0:1, 0:1], inc=s_t)
                p.wait("vector", s_t, tt)
                tmarks["pt"][which] = p.do("vector", "tensor_copy", out=dstcol[:, pr:pr + 1],
                                           in_=pt[which][:, 0:1], inc=s_v)
        V_ROWSX = p.do("vector", "tensor_tensor", out=EBCM[:], in0=VMC[:], in1=VCC[:],
                       op=ALU.subtract, inc=s_v)
        p.wait("scalar", s_v, V_ROWSX)
        p.do("scalar", "activation", out=EBM[:], in_=VMC[:], func=AF.Exp, scale=-1.0)
        p.do("scalar", "activation", out=EBCV[:], in_=VCC[:], func=AF.Exp, scale=-1.0)
        A_EBM = p.do("scalar", "activation", out=EBCM[:], in_=EBCM[:], func=AF.Exp, inc=s_a)
        if j >= 2:
            p.wait("scalar", s_v, VCH[j - 2])
        for pr in range(8):
            p.do("scalar", "activation", out=Ss_t[pr][par][:], in_=S_t[b][pr][:],
                 func=AF.Identity, scale=EBM[:, pr:pr + 1],
                 inc=s_a if pr == 7 else None)
        A_SS = s_a.n

        p.wait("tensor", s_v, V_FM)
        p.wait("tensor", s_a, A_SS)
        for h in range(H):
            pr, hi = h // 2, h % 2
            hb = h % 2
            AM = am_sb[hb]
            T_A = p.do("tensor", "matmul", pt[hb][:, 0:128],
                       KTF[64 * hi:64 * (hi + 1), 128 * pr:128 * (pr + 1)],
                       QTF[64 * hi:64 * (hi + 1), 128 * pr:128 * (pr + 1)],
                       start=True, stop=True, inc=s_t)
            p.wait("vector", s_t, T_A)
            p.do("vector", "memset", AM[:], 0.0)
            V_AM = p.do("vector", "copy_predicated", out=AM[:], mask=cn["mask_st"][:],
                        data=pt[hb][:, 0:128], inc=s_v)
            p.wait("tensor", s_v, V_AM)
            p.do("tensor", "matmul", pt[hb][:, 128:192], AM[:], VB[:, 64 * h:64 * (h + 1)],
                 start=True, stop=False)
            p.do("tensor", "matmul", pt[hb][:, 128:192],
                 QTF[64 * hi:64 * (hi + 1), 128 * pr:128 * (pr + 1)],
                 Ss_t[pr][par][64 * hi:64 * (hi + 1), :], start=False, stop=True)
            T_OP = p.do("tensor", "matmul", pq[0][64 * hi:64 * (hi + 1), 0:64],
                        KTT[:, 64 * h:64 * (h + 1)], VB[:, 64 * h:64 * (h + 1)],
                        start=True, stop=True, inc=s_t)
            p.wait("vector", s_t, T_OP)
            p.do("vector", "tensor_tensor", out=OG[:, 64 * h:64 * (h + 1)],
                 in0=pt[hb][:, 128:192], in1=GG[:, 64 * h:64 * (h + 1)], op=ALU.mult,
                 inc=s_v)
            if hi == 1:
                p.do("vector", "tensor_scalar_mul", out=Sn_t[pr][par][:],
                     in0=S_t[b][pr][:], scalar1=EBCV[:, pr:pr + 1])
                p.do("vector", "scalar_tensor_tensor", out=S_t[b][pr][:],
                     in0=pq[0][:, 0:64], scalar=EBCM[:, pr:pr + 1], in1=Sn_t[pr][par][:],
                     op0=ALU.mult, op1=ALU.add, inc=s_v)
        VHEADS[j] = s_v.n
        V_HEADS = s_v.n
        tmarks["pt"][0] = V_HEADS
        tmarks["pt"][1] = V_HEADS

        p.wait("scalar", s_v, V_HEADS)
        for h in range(H):
            p.do("scalar", "activation", out=sink[:, 0:64], in_=OG[:, 64 * h:64 * (h + 1)],
                 func=AF.Square, accum_out=ST[:, 8 + h:9 + h])
            p.do("scalar", "activation", out=ST[:, 8 + h:9 + h], in_=ST[:, 8 + h:9 + h],
                 func=AF.Ln, scale=1.0 / D, bias=RMS_EPS)
            A_RMS = p.do("scalar", "activation", out=ST[:, 8 + h:9 + h],
                         in_=ST[:, 8 + h:9 + h], func=AF.Exp, scale=-0.5, inc=s_a)
            p.wait("vector", s_a, A_RMS)
            p.do("vector", "tensor_scalar_mul", out=OGB[:, 64 * h:64 * (h + 1)],
                 in0=OG[:, 64 * h:64 * (h + 1)], scalar1=ST[:, 8 + h:9 + h],
                 inc=s_v if h == H - 1 else None)
        V_OGB = s_v.n
        p.wait("tensor", s_v, V_OGB)
        V_OGFM = trans_evac([OGB[:, 128 * k:128 * (k + 1)] for k in range(8)],
                            [OGF[:, 128 * k:128 * (k + 1)] for k in range(8)],
                            cn["eye_f"][:], "pb")
        p.wait("sync", s_t, T_V)
        p.wait("sync", s_wos, 8 * 16)
        p.do("sync", "dma_start", out=wst[:].rearrange("p (k n) -> p k n", n=1024),
             in_=io["wod_d"].rearrange("(k p) n -> p k n", p=128), inc=s_wv, inck=16)
        WO_L = s_wv.n
        p.wait("tensor", s_v, V_OGFM)
        p.wait("tensor", s_wv, WO_L)
        for half in range(2):
            for k in range(8):
                p.do("tensor", "matmul", pq[half][:], OGF[:, 128 * k:128 * (k + 1)],
                     wo_b[:, 1024 * k + 512 * half:1024 * k + 512 * (half + 1)],
                     start=(k == 0), stop=(k == 7),
                     inc=s_t if (half == 1 and k == 7) else None)
        T_WO = s_t.n
        TWOm[j] = T_WO
        p.wait("scalar", s_t, T_WO)
        if j >= 1:
            p.wait("scalar", s_st, STM[j - 1])
        p.do("scalar", "activation", out=AT[:, 0:512], in_=pq[0][:], func=AF.Identity,
             accum_out=ST[:, 0:1])
        p.do("scalar", "activation", out=AT[:, 512:1024], in_=pq[1][:], func=AF.Identity,
             accum_out=ST[:, 1:2])
        A_AT = p.do("scalar", "activation", out=sink[:], in_=AT[:], func=AF.Square,
                    accum_out=ST[:, 2:3], inc=s_a)
        AHEAD[j] = A_AT
        p.wait("vector", s_a, A_AT)
        p.do("vector", "tensor_scalar", out=ST[:, 3:4], in0=ST[:, 0:1],
             scalar1=ST[:, 1:2], scalar2=1.0 / E, op0=ALU.add, op1=ALU.mult)
        p.do("vector", "tensor_scalar", out=ST[:, 4:5], in0=ST[:, 3:4], scalar1=ST[:, 3:4],
             scalar2=None, op0=ALU.mult)
        V_ATV = p.do("vector", "tensor_scalar", out=ST[:, 5:6], in0=ST[:, 2:3],
                     scalar1=1.0 / E, scalar2=ST[:, 4:5], op0=ALU.mult, op1=ALU.subtract,
                     inc=s_v)
        p.wait("scalar", s_v, V_ATV)
        p.do("scalar", "activation", out=ST[:, 6:7], in_=ST[:, 5:6], func=AF.Ln, bias=LN_EPS)
        A_ATR = p.do("scalar", "activation", out=ST[:, 7:8], in_=ST[:, 6:7],
                     func=AF.Exp, scale=-0.5, inc=s_a)
        p.wait("vector", s_a, A_ATR)
        p.do("vector", "tensor_scalar", out=AT[:], in0=AT[:], scalar1=ST[:, 3:4],
             scalar2=ST[:, 7:8], op0=ALU.subtract, op1=ALU.mult)
        V_X1 = p.do("vector", "tensor_tensor", out=AT[:], in0=AT[:], in1=X[:], op=ALU.add,
                    inc=s_v)
        p.wait("scalar", s_v, V_X1)
        p.do("scalar", "dma_start",
             out=io["x1_d"].rearrange("(r p) n -> r p n", p=128)[lcol], in_=AT[:],
             inc=s_st, inck=16)
        X1ST = s_st.n
        p.do("scalar", "activation", out=sink[:], in_=AT[:], func=AF.Identity,
             accum_out=ST[:, 0:1])
        A_ST2 = p.do("scalar", "activation", out=sink[:], in_=AT[:], func=AF.Square,
                     accum_out=ST[:, 1:2], inc=s_a)
        p.wait("vector", s_a, A_ST2)
        p.do("vector", "tensor_scalar_mul", out=ST[:, 2:3], in0=ST[:, 0:1], scalar1=1.0 / E)
        p.do("vector", "tensor_scalar", out=ST[:, 3:4], in0=ST[:, 2:3], scalar1=ST[:, 2:3],
             scalar2=None, op0=ALU.mult)
        V_ST2 = p.do("vector", "tensor_scalar", out=ST[:, 4:5], in0=ST[:, 1:2],
                     scalar1=1.0 / E, scalar2=ST[:, 3:4], op0=ALU.mult, op1=ALU.subtract,
                     inc=s_v)
        p.wait("scalar", s_v, V_ST2)
        p.do("scalar", "activation", out=ST[:, 5:6], in_=ST[:, 4:5], func=AF.Ln, bias=LN_EPS)
        A_R2 = p.do("scalar", "activation", out=ST[:, 6:7], in_=ST[:, 5:6],
                    func=AF.Exp, scale=-0.5, inc=s_a)
        p.wait("vector", s_a, A_R2)
        if j >= 1:
            p.wait("vector", s_st, STM[j - 1])
            p.wait("vector", s_g, GXN[j - 1])
        p.wait("vector", s_st, X1ST)
        V_XN2 = p.do("vector", "tensor_scalar", out=XN2[:], in0=AT[:], scalar1=ST[:, 2:3],
                     scalar2=ST[:, 6:7], op0=ALU.subtract, op1=ALU.mult, inc=s_v)
        p.wait("scalar", s_v, V_XN2)
        p.do("scalar", "dma_start",
             out=io["xn2_d"].rearrange("(r p) n -> r p n", p=128)[lcol], in_=XN2[:],
             inc=s_st, inck=16)
        STM[j] = s_st.n
        p.wait("gpsimd", s_v, V_XN2)
        p.do("gpsimd", "dma_start",
             out=io["xn2b_d"].rearrange("(r p) n -> r p n", p=128)[lcol], in_=XN2[:],
             inc=s_g, inck=16)
        GXN[j] = s_g.n
        p.wait("tensor", s_v, V_XN2)
        V_XN2FM = trans_evac([XN2[:, 128 * k:128 * (k + 1)] for k in range(8)],
                             [XN2F[:, 128 * k:128 * (k + 1)] for k in range(8)],
                             cn["eye_f"][:], "pt")
        p.wait("tensor", s_v, V_XN2FM)
        for k in range(8):
            p.do("tensor", "matmul", pt[1][:, 0:8], XN2F[:, 128 * k:128 * (k + 1)],
                 rwT[:, 8 * k:8 * (k + 1)], start=(k == 0), stop=False)
        T_LOG = p.do("tensor", "matmul", pt[1][:, 0:8], ones1f[:], rbrow[:],
                     start=False, stop=True, inc=s_t)
        p.wait("vector", s_t, T_LOG)
        p.do("vector", "tensor_reduce", out=SM[:, 0:1], in_=pt[1][:, 0:8], axis=AX.X,
             op=ALU.max)
        p.do("vector", "tensor_scalar_mul", out=SM[:, 0:1], in0=SM[:, 0:1], scalar1=-1.0)
        V_LCP = p.do("vector", "tensor_copy", out=T8[:], in_=pt[1][:, 0:8], inc=s_v)
        tmarks["pt"][1] = V_LCP
        p.wait("scalar", s_v, V_LCP)
        A_SM = p.do("scalar", "activation", out=T8[:], in_=T8[:], func=AF.Exp,
                    bias=SM[:, 0:1], inc=s_a)
        p.wait("vector", s_a, A_SM)
        p.do("vector", "tensor_reduce", out=SM[:, 1:2], in_=T8[:], axis=AX.X, op=ALU.add)
        p.do("vector", "reciprocal", out=SM[:, 2:3], in_=SM[:, 1:2])
        p.do("vector", "tensor_scalar_mul", out=T8[:], in0=T8[:], scalar1=SM[:, 2:3])
        p.do("vector", "max", out=SM[:], in_=T8[:])
        p.do("vector", "tensor_single_scalar", out=ST[:, 0:8], in_=T8[:],
             scalar=SM[:, 0:1], op=ALU.is_ge)
        p.do("vector", "scalar_tensor_tensor", out=ST[:, 0:8], in0=ST[:, 0:8],
             scalar=-BIG, in1=cn["iota8"][:], op0=ALU.mult, op1=ALU.add)
        p.do("vector", "tensor_reduce", out=e0sb[:, lcol:lcol + 1], in_=ST[:, 0:8],
             axis=AX.X, op=ALU.min)
        p.do("vector", "tensor_scalar_add", out=e0sb[:, lcol:lcol + 1],
             in0=e0sb[:, lcol:lcol + 1], scalar1=BIG)
        p.do("vector", "tensor_single_scalar", out=ST[:, 0:8], in_=T8[:],
             scalar=SM[:, 1:2], op=ALU.is_ge)
        p.do("vector", "scalar_tensor_tensor", out=ST[:, 0:8], in0=ST[:, 0:8],
             scalar=-BIG, in1=cn["iota8"][:], op0=ALU.mult, op1=ALU.add)
        p.do("vector", "tensor_single_scalar", out=ST[:, 8:16], in_=cn["iota8"][:],
             scalar=e0sb[:, lcol:lcol + 1], op=ALU.is_equal)
        p.do("vector", "scalar_tensor_tensor", out=ST[:, 0:8], in0=ST[:, 8:16],
             scalar=2 * BIG, in1=ST[:, 0:8], op0=ALU.mult, op1=ALU.add)
        p.do("vector", "tensor_reduce", out=e1sb[:, lcol:lcol + 1], in_=ST[:, 0:8],
             axis=AX.X, op=ALU.min)
        p.do("vector", "tensor_scalar_add", out=e1sb[:, lcol:lcol + 1],
             in0=e1sb[:, lcol:lcol + 1], scalar1=BIG)
        p.do("vector", "tensor_scalar", out=SM[:, 3:4], in0=SM[:, 0:1],
             scalar1=SM[:, 1:2], scalar2=RT_EPS, op0=ALU.add, op1=ALU.add)
        p.do("vector", "reciprocal", out=SM[:, 4:5], in_=SM[:, 3:4])
        p.do("vector", "tensor_scalar", out=w0sb[:, lcol:lcol + 1], in0=SM[:, 0:1],
             scalar1=SM[:, 4:5], scalar2=None, op0=ALU.mult)
        VCH[j] = p.do("vector", "tensor_scalar", out=w1sb[:, lcol:lcol + 1], in0=SM[:, 1:2],
                      scalar1=SM[:, 4:5], scalar2=None, op0=ALU.mult, inc=s_v)

    p.wait("scalar", s_v, VCH[-1])
    for idx, arr in enumerate([e0sb, e1sb, w0sb, w1sb]):
        p.do("scalar", "dma_start",
             out=io["route_d"].rearrange("(a p) n -> a p n", p=128)[idx], in_=arr[:],
             inc=s_st, inck=16)
    ST_END = s_st.n
    G_END = s_g.n

    return dict(s_x=s_x, s_v=s_v, s_a=s_a, s_t=s_t, s_st=s_st, s_g=s_g, onescol=onescol,
                ST_END=ST_END, G_END=G_END, VCH=VCH, cn=cn, ones1f=ones1f,
                coreinfo=coreinfo, pq=pq, pk=pk, pt=pt, pb=pb, sink=sink,
                e0sb=e0sb, e1sb=e1sb, w0sb=w0sb, w1sb=w1sb, rwT=rwT)


def build_routing(p, io, at, sbr, fused):
    nc = p.nc
    cn = at["cn"]; ones1f = at["ones1f"]; coreinfo = at["coreinfo"]
    pq, pk, pt, pb = at["pq"], at["pk"], at["pt"], at["pb"]
    s_v, s_a, s_t, s_g, s_st = at["s_v"], at["s_a"], at["s_t"], at["s_g"], at["s_st"]
    e0sb, e1sb, w0sb, w1sb = at["e0sb"], at["e1sb"], at["w0sb"], at["w1sb"]

    if fused:
        s_c1 = p.sem("s_c1")
        p.wait("gpsimd", s_st, at["ST_END"])
        p.do("gpsimd", "collective_compute", "AllGather", ALU.bypass,
             [list(range(NCORE))], [io["route_d"].opt()], [io["routeall_d"].opt()],
             inc=s_c1)
        p.wait("sync", s_c1, s_c1.n)

    s_ld = p.sem("s_ld")
    e0a = sbr("e0a", [128, 512], F32); e1a = sbr("e1a", [128, 512], F32)
    w0a = sbr("w0a", [128, 512], F32); w1a = sbr("w1a", [128, 512], F32)
    ra = io["routeall_d"].rearrange("(r a p) n -> r a p n", a=4, p=128)
    for r in range(NCORE):
        for ai, dst in ((0, e0a), (1, e1a), (2, w0a), (3, w1a)):
            p.do("sync", "dma_start", out=dst[:, 64 * r:64 * (r + 1)], in_=ra[r, ai],
                 inc=s_ld, inck=16)
    LD_RA = s_ld.n

    import os as _os
    _STAGE = int(_os.environ.get("MK_ROUTE_STAGE", "99"))
    oh = [sbr(f"oh{e}", [128, 1024], BF16) for e in range(8)]
    vm0 = sbr("vm0", [128, 512], F32); vm1 = sbr("vm1", [128, 512], F32)
    eqt = sbr("eqt", [128, 512], F32)
    POS = sbr("POS", [128, 8 * 1024], F32)
    posown = sbr("posown", [128, 1024], F32)
    ohf = sbr("ohf", [128, 1024], F32)
    tmpv = sbr("tmpv", [128, 1024], F32)
    csrow = sbr("csrow", [1, 1024], F32)
    ones_row = sbr("ones_row", [1, 1024], F32)
    offrow = sbr("offrow", [1, 1024], F32)
    ones128f = sbr("ones128f", [128, 1], F32)
    ones128b = sbr("ones128b", [128, 1], BF16)
    ownposT = sbr("ownposT", [128, 128], F32)
    postmp = sbr("postmp", [128, 1024], F32)
    posloc = sbr("posloc", [128, 128], F32)
    tokloc = sbr("tokloc", [128, 64], F32)
    toku = sbr("toku", [128, 64], U32)
    sc0 = sbr("sc0", [128, 64], F32); sc1r = sbr("sc1r", [128, 64], F32)
    scu = sbr("scu", [128, 128], U32)
    gsl = sbr("gsl", [128, 128], F32)
    ps_ = sbr("ps_", [128, 64], F32); st0_ = sbr("st0_", [128, 64], F32)
    st1_ = sbr("st1_", [128, 64], F32)
    gslu = sbr("gslu", [128, 128], U32)
    weff = sbr("weff", [128, 128], F32)
    kvm = sbr("kvm", [128, 64], F32)
    zt = sbr("zt", [128, MAPC], F32)
    dumpc = sbr("dumpc", [128, 1], F32)
    idxc = sbr("idxc", [128, 1], U32)
    idxf = sbr("idxf", [128, 1], F32)

    p.wait("vector", s_ld, LD_RA)
    p.do("vector", "tensor_single_scalar", out=dumpc[:], in_=cn["iotacol"][:],
         scalar=float(CAPP * 8), op=ALU.add)
    p.do("vector", "memset", ones128f[:], 1.0)
    p.do("vector", "memset", ones128b[:], 1.0)
    p.do("vector", "memset", ones_row[:], 1.0)
    p.do("vector", "memset", zt[:], 0.0)
    p.do("vector", "tensor_single_scalar", out=vm0[:], in_=w0a[:], scalar=RT_EPS, op=ALU.is_gt)
    p.do("vector", "tensor_single_scalar", out=vm1[:], in_=w1a[:], scalar=RT_EPS,
         op=ALU.is_gt, inc=s_v)
    for e in range(8):
        p.do("vector", "tensor_single_scalar", out=eqt[:], in_=e0a[:], scalar=float(e),
             op=ALU.is_equal)
        p.do("vector", "tensor_tensor", out=oh[e][:, 0:512], in0=eqt[:], in1=vm0[:],
             op=ALU.mult)
        p.do("vector", "tensor_single_scalar", out=eqt[:], in_=e1a[:], scalar=float(e),
             op=ALU.is_equal)
        p.do("vector", "tensor_tensor", out=oh[e][:, 512:1024], in0=eqt[:], in1=vm1[:],
             op=ALU.mult, inc=s_v)
    V_OH = s_v.n
    s_gz = p.sem("s_gz")
    p.wait("gpsimd", s_v, V_OH)
    GZ = p.do("gpsimd", "dma_start", out=io["map_d"].rearrange("(p n) -> p n", p=128),
              in_=zt[:], inc=s_gz, inck=16)

    if _STAGE < 2:
        return dict()
    _SUB = int(_os.environ.get("MK_ROUTE_SUB", "99"))
    _NE = int(_os.environ.get("MK_ROUTE_NE", "8"))
    p.wait("tensor", s_v, V_OH)
    for e in range(_NE):
        for half in range(2):
            p.do("tensor", "matmul", pt[half][0:1, 0:512], ones128b[:],
                 oh[e][:, 512 * half:512 * (half + 1)], start=True, stop=True,
                 inc=s_t if half == 1 else None)
        T_CS = s_t.n
        p.wait("vector", s_t, T_CS)
        p.do("vector", "tensor_copy", out=csrow[0:1, 0:512], in_=pt[0][0:1, 0:512])
        V_CS = p.do("vector", "tensor_copy", out=csrow[0:1, 512:1024],
                    in_=pt[1][0:1, 0:512], inc=s_v)
        p.wait("vector", s_t, 0)
        p.do("vector", "tensor_tensor_scan", out=offrow[:], data0=ones_row[:],
             data1=csrow[:], initial=0.0, op0=ALU.mult, op1=ALU.add)
        V_OFR = p.do("vector", "tensor_tensor", out=offrow[:], in0=offrow[:],
                     in1=csrow[:], op=ALU.subtract, inc=s_v)
        if _SUB < 7:
            continue
        p.wait("tensor", s_v, V_OFR)
        for half in range(2):
            p.do("tensor", "matmul", pq[half][:], cn["u128b"][:],
                 oh[e][:, 512 * half:512 * (half + 1)], start=True, stop=False)
            p.do("tensor", "matmul", pq[half][:], ones1f[:],
                 offrow[0:1, 512 * half:512 * (half + 1)], start=False, stop=True,
                 inc=s_t if half == 1 else None)
        T_POS = s_t.n
        p.wait("scalar", s_t, T_POS)
        p.do("scalar", "activation", out=POS[:, 1024 * e:1024 * e + 512], in_=pq[0][:],
             func=AF.Identity, bias=-1.0)
        A_POS = p.do("scalar", "activation", out=POS[:, 1024 * e + 512:1024 * (e + 1)],
                     in_=pq[1][:], func=AF.Identity, bias=-1.0, inc=s_a)
        p.wait("vector", s_a, A_POS)
        p.wait("tensor", s_a, A_POS)
        if "dbg_csoff" in io:
            p.do("scalar", "dma_start",
                 out=io["dbg_csoff"].rearrange("(e a) n -> e a n", a=2)[e, 0:1, :],
                 in_=csrow[:], inc=s_st, inck=16)
            p.do("scalar", "dma_start",
                 out=io["dbg_csoff"].rearrange("(e a) n -> e a n", a=2)[e, 1:2, :],
                 in_=offrow[:], inc=s_st, inck=16)

    if _STAGE < 3:
        return dict()
    for e in range(8):
        p.do("vector", "tensor_copy", out=ohf[:], in_=oh[e][:])
        if e == 0:
            p.do("vector", "tensor_tensor", out=posown[:], in0=ohf[:],
                 in1=POS[:, 0:1024], op=ALU.mult)
        else:
            p.do("vector", "tensor_tensor", out=tmpv[:], in0=ohf[:],
                 in1=POS[:, 1024 * e:1024 * (e + 1)], op=ALU.mult)
            p.do("vector", "tensor_tensor", out=posown[:], in0=posown[:], in1=tmpv[:],
                 op=ALU.add)
    V_PO = p.do("vector", "engine_nop", inc=s_v)
    if "dbg_posown" in io:
        p.wait("scalar", s_v, V_PO)
        p.do("scalar", "dma_start", out=io["dbg_posown"], in_=posown[:], inc=s_st, inck=16)
        p.do("scalar", "dma_start", out=io["dbg_pos"], in_=POS[:], inc=s_st, inck=16)
    if int(_os.environ.get("MK_ROUTE_SUB3", "9")) < 2:
        return dict()
    p.wait("tensor", s_v, V_PO)
    s_pt = p.sem("s_pt")
    evm = [0, 0]
    for k in range(8):
        if evm[k % 2]:
            p.wait("tensor", s_v, evm[k % 2])
        tt = p.do("tensor", "transpose", pt[k % 2][:, 0:128],
                  posown[:, 128 * k:128 * (k + 1)], cn["eye_f"][:], inc=s_t)
        p.wait("vector", s_t, tt)
        vv = p.do("vector", "tensor_copy", out=postmp[:, 128 * k:128 * (k + 1)],
                  in_=pt[k % 2][:, 0:128], inc=s_v)
        evm[k % 2] = vv
    V_PT = s_v.n
    if "dbg_postmp" in io:
        p.wait("scalar", s_v, V_PT)
        p.do("scalar", "dma_start", out=io["dbg_postmp"], in_=postmp[:], inc=s_st, inck=16)
    p.wait("gpsimd", s_v, V_PT)
    p.do("gpsimd", "dma_start",
         out=io["posownT_d"].rearrange("(k p) n -> p k n", p=128),
         in_=postmp[:].rearrange("p (k n) -> p k n", n=128), inc=s_pt, inck=16)
    PT_ST = s_pt.n
    if int(_os.environ.get("MK_ROUTE_SUB3", "9")) < 3:
        return dict()
    _w1 = p.do("vector", "tensor_single_scalar", out=idxf[:], in_=cn["iotacol"][:],
               scalar=63.5, op=ALU.is_gt, inc=s_v)
    p.wait("vector", s_v, _w1)
    _w2 = p.do("vector", "scalar_tensor_tensor", out=idxf[:], in0=idxf[:], scalar=448.0,
               in1=cn["iotacol"][:], op0=ALU.mult, op1=ALU.add, inc=s_v)
    p.wait("vector", s_v, _w2)
    _w3 = p.do("vector", "tensor_tensor", out=idxf[:], in0=idxf[:], in1=coreinfo[:, 2:3],
               op=ALU.add, inc=s_v)
    p.wait("vector", s_v, _w3)
    V_IDX = p.do("vector", "tensor_copy", out=idxc[:], in_=idxf[:], inc=s_v)
    s_go = p.sem("s_go")
    if "dbg_idx" in io:
        p.wait("scalar", s_v, V_IDX)
        p.do("scalar", "dma_start", out=io["dbg_idx"], in_=idxc[:], inc=s_st, inck=16)
    p.wait("gpsimd", s_v, V_IDX)
    p.wait("gpsimd", s_pt, PT_ST)
    G_OWN = p.do("gpsimd", "indirect_dma_start", out=ownposT[:], out_offset=None,
                 in_=io["posownT_d"], in_offset=IOA(idxc[:], 0), inc=s_go, inck=16)
    if int(_os.environ.get("MK_ROUTE_SUB3", "9")) < 4:
        return dict()
    if "dbg_ownposT" in io:
        p.wait("scalar", s_go, G_OWN)
        p.do("scalar", "dma_start", out=io["dbg_ownposT"], in_=ownposT[:], inc=s_st, inck=16)
    p.wait("tensor", s_go, G_OWN)
    T_PL = p.do("tensor", "transpose", pt[0][:, 0:128], ownposT[:], cn["eye_f"][:], inc=s_t)
    p.wait("vector", s_t, T_PL)
    V_PL = p.do("vector", "tensor_copy", out=posloc[:], in_=pt[0][:, 0:128], inc=s_v)
    if "dbg_posloc" in io:
        p.wait("scalar", s_v, V_PL)
        p.do("scalar", "dma_start", out=io["dbg_posloc"], in_=posloc[:], inc=s_st, inck=16)

    if _STAGE < 4:
        return dict()
    s_gi = p.sem("s_gi")
    GI = p.do("gpsimd", "iota", out=toku[:], pattern=[[128, 64]], base=0,
              channel_multiplier=1, inc=s_gi, inck=1)
    p.wait("vector", s_gi, GI)
    p.do("vector", "tensor_copy", out=tokloc[:], in_=toku[:])
    V_TK = p.do("vector", "tensor_scalar_add", out=tokloc[:], in0=tokloc[:],
                scalar1=coreinfo[:, 1:2], inc=s_v)

    for rr, (esb, wsb, vmloc) in enumerate(((e0sb, w0sb, None), (e1sb, w1sb, None))):
        PL = posloc[:, 64 * rr:64 * (rr + 1)]
        p.do("vector", "tensor_single_scalar", out=kvm[:], in_=PL, scalar=float(CAP),
             op=ALU.is_lt)
        p.do("vector", "tensor_single_scalar", out=eqt[:, 0:64], in_=wsb[:],
             scalar=RT_EPS, op=ALU.is_gt)
        p.do("vector", "tensor_tensor", out=kvm[:], in0=kvm[:], in1=eqt[:, 0:64],
             op=ALU.mult)
        p.do("vector", "tensor_tensor", out=weff[:, 64 * rr:64 * (rr + 1)], in0=wsb[:],
             in1=kvm[:], op=ALU.mult)
        p.do("vector", "scalar_tensor_tensor", out=sc0[:], in0=esb[:], scalar=float(CAPP),
             in1=PL, op0=ALU.mult, op1=ALU.add)
        p.do("vector", "tensor_single_scalar", out=ps_[:], in_=PL,
             scalar=float(CAPP - 1), op=ALU.min)
        p.do("vector", "tensor_single_scalar", out=st0_[:], in_=ps_[:], scalar=2560.0,
             op=ALU.is_ge)
        p.do("vector", "tensor_single_scalar", out=st1_[:], in_=ps_[:], scalar=5120.0,
             op=ALU.is_ge)
        p.do("vector", "tensor_tensor", out=st0_[:], in0=st0_[:], in1=st1_[:], op=ALU.add)
        p.do("vector", "tensor_single_scalar", out=st1_[:], in_=ps_[:], scalar=7680.0,
             op=ALU.is_ge)
        p.do("vector", "tensor_tensor", out=st0_[:], in0=st0_[:], in1=st1_[:], op=ALU.add)
        p.do("vector", "scalar_tensor_tensor", out=st1_[:], in0=st1_[:], scalar=-384.0,
             in1=esb[:], op0=ALU.mult, op1=ALU.mult)
        p.do("vector", "scalar_tensor_tensor", out=st0_[:], in0=st0_[:], scalar=17920.0,
             in1=st1_[:], op0=ALU.mult, op1=ALU.add)
        p.do("vector", "scalar_tensor_tensor", out=st1_[:], in0=esb[:], scalar=2560.0,
             in1=ps_[:], op0=ALU.mult, op1=ALU.add)
        p.do("vector", "tensor_tensor", out=st0_[:], in0=st0_[:], in1=st1_[:], op=ALU.add)
        p.do("vector", "tensor_tensor", out=gsl[:, 64 * rr:64 * (rr + 1)], in0=st0_[:],
             in1=kvm[:], op=ALU.mult)
        p.do("vector", "tensor_scalar", out=sc1r[:], in0=sc0[:], scalar1=dumpc[:, 0:1],
             scalar2=None, op0=ALU.subtract)
        p.do("vector", "tensor_tensor", out=sc1r[:], in0=sc1r[:], in1=kvm[:], op=ALU.mult)
        p.do("vector", "tensor_scalar", out=sc1r[:], in0=sc1r[:], scalar1=dumpc[:, 0:1],
             scalar2=None, op0=ALU.add)
        p.do("vector", "tensor_copy", out=scu[:, 64 * rr:64 * (rr + 1)], in_=sc1r[:])
    p.do("vector", "tensor_copy", out=gslu[:], in_=gsl[:])
    V_SC = p.do("vector", "engine_nop", inc=s_v)

    if _STAGE < 5:
        return dict()
    p.wait("gpsimd", s_v, V_SC)
    p.wait("gpsimd", s_v, V_TK)
    p.wait("gpsimd", s_gz, GZ)
    mapv = io["map_d"].rearrange("(s a) -> s a", a=1)
    scat_marks = []
    for col in range(128):
        if col >= 16:
            p.wait("gpsimd", s_g, scat_marks[col - 16])
        m = p.do("gpsimd", "indirect_dma_start", out=mapv,
                 out_offset=IOA(scu[:, col:col + 1], 0),
                 in_=tokloc[:, (col % 64):(col % 64) + 1] if col < 64 else tokloc[:, (col - 64):(col - 63)],
                 in_offset=None, inc=s_g, inck=16)
        scat_marks.append(m)
    G_SCAT = s_g.n
    s_rt = p.sem("s_rt")
    p.wait("scalar", s_v, V_SC)
    p.do("scalar", "dma_start", out=io["weff_d"], in_=weff[:], inc=s_rt, inck=16)
    p.do("scalar", "dma_start", out=io["gsl_d"], in_=gslu[:], inc=s_rt, inck=16)
    RT_ST = s_rt.n

    if fused:
        s_c2 = p.sem("s_c2")
        p.wait("gpsimd", s_g, G_SCAT)
        p.do("gpsimd", "collective_compute", "AllGather", ALU.bypass,
             [list(range(NCORE))], [io["map_d"].opt()], [io["maps8_d"].opt()], inc=s_c2)
        at["S_C2"] = (s_c2, s_c2.n)
    at["G_SCAT"] = G_SCAT
    at["RT_ST"] = (s_rt, RT_ST)
    return dict()


def build_expert(p, io, at, sbe, fused):
    nc = p.nc
    cn = at["cn"]; ones1f = at["ones1f"]; coreinfo = at["coreinfo"]
    pq, pk, pt, pb = at["pq"], at["pk"], at["pt"], at["pb"]
    s_v, s_a, s_t, s_g = at["s_v"], at["s_a"], at["s_t"], at["s_g"]
    s_w2 = p.sem("s_w2")
    s_y = p.sem("s_y")

    fc1T = sbe("fc1T", [128, 8 * 4096], BF16)
    fc2T = sbe("fc2T", [128, 32 * 1024], BF16)
    fc1bc = sbe("fc1bc", [128, 32], F32)
    fc1br = sbe("fc1br", [32, 128], F32)
    ls_c = sbe("ls_c", [128, 8], F32)
    biasrow = sbe("biasrow", [1, 1024], F32)
    mapr = sbe("mapr", [128, 128], F32)
    mapu = sbe("mapu", [128, 80], U32)
    midx = sbe("midx", [128, 1], U32)
    midxf = sbe("midxf", [128, 1], F32)
    midxf2 = sbe("midxf2", [128, 1], F32)
    _m8 = sbe("m8s", [128, MAPC], F32)
    m8s = [_m8, _m8]
    _tokb = sbe("tokb", [128, 4 * 1024], BF16)
    tokb = [_tokb, _tokb]
    _tokfm = sbe("tokfm", [128, 8 * 512], BF16)
    tokfm = [_tokfm, _tokfm]
    _hb = sbe("hbuf", [128, 32 * 512], BF16)
    hbuf = [_hb, _hb]
    _pctx = ExitStack()
    _stg = _pctx.enter_context(nc.sbuf_tensor(_uniq(nc, "stg"), [128, 2048], F32))
    stg = [_stg, _stg]
    _sgb = _pctx.enter_context(nc.sbuf_tensor(_uniq(nc, "stgb"), [128, 1024], BF16))
    stgb = [_sgb, _sgb]
    stgs = _pctx.enter_context(nc.sbuf_tensor(_uniq(nc, "stgs"), [128, 2048], BF16))
    ls_r = _pctx.enter_context(nc.sbuf_tensor(_uniq(nc, "ls_r"), [8, 128], F32))
    lsrow = _pctx.enter_context(nc.sbuf_tensor(_uniq(nc, "lsrow"), [1, 1024], F32))

    import os as _os
    ent_v, ent_a, ent_t, ent_g = s_v.n, s_a.n, s_t.n, s_g.n
    p.wait("vector", s_a, ent_a); p.wait("vector", s_t, ent_t)
    p.wait("scalar", s_v, ent_v); p.wait("scalar", s_t, ent_t)
    p.wait("tensor", s_v, ent_v); p.wait("tensor", s_a, ent_a)
    p.wait("gpsimd", s_v, ent_v); p.wait("gpsimd", s_t, ent_t)
    p.wait("sync", s_v, ent_v)

    fc1evac = [0] * 32
    s_f1 = p.sem("s_f1")
    pbm = [0, 0]
    for r in range(32):
        buf = stgb[r % 2]
        if r >= 1:
            p.wait("gpsimd", s_v, fc1evac[r - 1])
        p.do("gpsimd", "dma_start", out=buf[:],
             in_=io["fc1w"][128 * r:128 * (r + 1), :], inc=s_f1, inck=16)
        G_L = s_f1.n
        p.wait("tensor", s_f1, G_L)
        for k in range(8):
            tb = k % 2
            if pbm[tb]:
                p.wait("tensor", s_v, pbm[tb])
            tt = p.do("tensor", "transpose", pb[tb][:, 0:128],
                      buf[:, 128 * k:128 * (k + 1)], cn["eye_b"][:], inc=s_t)
            p.wait("vector", s_t, tt)
            pbm[tb] = p.do("vector", "tensor_copy",
                           out=fc1T[:, 4096 * k + 128 * r:4096 * k + 128 * (r + 1)],
                           in_=pb[tb][:, 0:128], inc=s_v)
        fc1evac[r] = s_v.n
    V_FC1 = s_v.n
    if int(_os.environ.get("MK_EXP_W", "9")) < 2:
        return dict()
    p.do("sync", "dma_start", out=fc1br[:], in_=io["fc1b"].rearrange("(k n) -> k n", n=128),
         inc=s_w2, inck=16)
    p.do("sync", "dma_start", out=ls_r[:], in_=io["ls"].rearrange("(k n) -> k n", n=128),
         inc=s_w2, inck=16)
    p.do("sync", "dma_start", out=lsrow[:], in_=io["ls"].rearrange("(a n) -> a n", a=1),
         inc=s_w2, inck=16)
    p.do("sync", "dma_start", out=biasrow[:], in_=io["fc2b"].rearrange("(a n) -> a n", a=1),
         inc=s_w2, inck=16)
    W2 = s_w2.n
    p.wait("tensor", s_w2, W2)
    tt = p.do("tensor", "transpose", pt[0][:, 0:32], fc1br[:], cn["eye_f"][0:32, 0:32],
              inc=s_t)
    p.wait("vector", s_t, tt)
    _m = p.do("vector", "tensor_copy", out=fc1bc[:], in_=pt[0][:, 0:32], inc=s_v)
    p.wait("tensor", s_v, _m)
    tt = p.do("tensor", "transpose", pt[0][:, 0:8], ls_r[:], cn["eye_f"][0:8, 0:8], inc=s_t)
    p.wait("vector", s_t, tt)
    p.do("vector", "tensor_copy", out=ls_c[:], in_=pt[0][:, 0:8])
    p.do("vector", "tensor_tensor", out=biasrow[:], in0=biasrow[:], in1=lsrow[:],
         op=ALU.mult, inc=s_v)
    V_BR = s_v.n
    if int(_os.environ.get("MK_EXP_W", "9")) < 3:
        return dict()
    fc2scale = 0
    prev_stgs_t = 0
    for rr2 in range(16):
        r, hh = rr2 // 2, rr2 % 2
        if rr2 >= 1:
            p.wait("sync", s_v, fc2scale)
        p.do("sync", "dma_start", out=stg[0][:],
             in_=io["fc2w"][128 * r:128 * (r + 1), 2048 * hh:2048 * (hh + 1)],
             inc=s_w2, inck=16)
        W_L = s_w2.n
        p.wait("vector", s_w2, W_L)
        if rr2 >= 1:
            p.wait("vector", s_t, prev_stgs_t)
        fc2scale = p.do("vector", "tensor_scalar_mul", out=stgs[:], in0=stg[0][:],
                        scalar1=ls_c[:, r:r + 1], inc=s_v)
        p.wait("tensor", s_v, fc2scale)
        for mm in range(16):
            m = 16 * hh + mm
            tb = mm % 2
            if pbm[tb]:
                p.wait("tensor", s_v, pbm[tb])
            tt = p.do("tensor", "transpose", pb[tb][:, 0:128],
                      stgs[:, 128 * mm:128 * (mm + 1)], cn["eye_b"][:], inc=s_t)
            prev_stgs_t = tt
            p.wait("vector", s_t, tt)
            pbm[tb] = p.do("vector", "tensor_copy",
                           out=fc2T[:, 1024 * m + 128 * r:1024 * m + 128 * (r + 1)],
                           in_=pb[tb][:, 0:128], inc=s_v)
    V_FC2 = s_v.n
    _pctx.close()
    _yt = sbe("ytile", [128, 1024], F32)
    ytile = [_yt, _yt]

    _ES = int(_os.environ.get("MK_EXP_STAGE", "99"))
    if _ES < 2:
        return dict()
    if fused:
        sc2, c2n = at["S_C2"]
        p.wait("sync", sc2, c2n)
        p.wait("gpsimd", sc2, c2n)
        scx, cxn = at["S_CX"]
        p.wait("gpsimd", scx, cxn)
    s_m = p.sem("s_m")
    m8 = io["maps8_d"].rearrange("(r p n) -> r p n", p=128, n=MAPC)
    p.do("gpsimd", "dma_start", out=m8s[0][:], in_=m8[0], inc=s_m, inck=16)
    for r in range(1, 8):
        p.wait("gpsimd", s_m, s_m.n)
        p.do("gpsimd", "dma_start", out=m8s[0][:], in_=m8[r], accum_op=ALU.add,
             inc=s_m, inck=16)
    p.wait("vector", s_m, s_m.n)
    V_MS = p.do("vector", "engine_nop", inc=s_v)
    s_ms = p.sem("s_ms")
    p.wait("scalar", s_v, V_MS)
    p.do("scalar", "dma_start", out=io["mapsum_d"].rearrange("(p n) -> p n", p=128),
         in_=m8s[0][:], inc=s_ms, inck=16)
    MS_ST = s_ms.n
    if "dbg_ci2" in io:
        sdbg2 = p.sem("sdbg2")
        p.do("scalar", "dma_start", out=io["dbg_ci2"], in_=coreinfo[:], inc=sdbg2, inck=16)
        p.wait("scalar", sdbg2, 16)
    p.do("vector", "tensor_single_scalar", out=midxf[:], in_=cn["iotacol"][:],
         scalar=76.0, op=ALU.min)
    p.do("vector", "tensor_copy", out=midxf2[:], in_=coreinfo[:, 3:4])
    p.do("vector", "tensor_tensor", out=midxf[:], in0=midxf[:], in1=midxf2[:], op=ALU.add)
    V_MI = p.do("vector", "tensor_copy", out=midx[:], in_=midxf[:], inc=s_v)
    if "dbg_midx" in io:
        p.wait("scalar", s_v, V_MI)
        sdbg = p.sem("sdbg")
        p.do("scalar", "dma_start", out=io["dbg_midx"], in_=midx[:], inc=sdbg, inck=16)
        p.do("scalar", "dma_start", out=io["dbg_ci"], in_=coreinfo[:], inc=sdbg, inck=16)
        p.wait("scalar", sdbg, 32)
    p.wait("gpsimd", s_v, V_MI)
    p.wait("gpsimd", s_ms, MS_ST)
    s_gm = p.sem("s_gm")
    G_MR = p.do("gpsimd", "indirect_dma_start", out=mapr[:], out_offset=None,
                in_=io["mapsum_d"].rearrange("(r n) -> r n", n=128),
                in_offset=IOA(midx[:], 0), inc=s_gm, inck=16)
    p.wait("tensor", s_gm, G_MR)
    tt = p.do("tensor", "transpose", pt[0][:, 0:128], mapr[:], cn["eye_f"][:], inc=s_t)
    p.wait("vector", s_t, tt)
    V_MU = p.do("vector", "tensor_copy", out=mapu[:, 0:77], in_=pt[0][:, 0:77], inc=s_v)

    if _ES < 3:
        return dict()
    _NTI = int(_os.environ.get("MK_EXP_NTI", "20"))
    ysl = io["ylocal_d"].rearrange("(r p) n -> r p n", p=128)
    ytm = [0, 0]
    last_pq = [0, 0]
    last_pk = [0, 0]
    s_tk = p.sem("s_tk")
    for ti in range(_NTI):
        nsub = 4 if ti < 19 else 1
        tb4 = tokb[ti % 2]; tf4 = tokfm[ti % 2]; hb4 = hbuf[ti % 2]
        p.wait("gpsimd", s_v, V_MU)
        if ti >= 1:
            p.wait("gpsimd", s_v, at["V_TOKT"][-1])
        for sub in range(nsub):
            p.do("gpsimd", "indirect_dma_start", out=tb4[:, 1024 * sub:1024 * (sub + 1)],
                 out_offset=None, in_=io["xnall_d"],
                 in_offset=IOA(mapu[:, 4 * ti + sub:4 * ti + sub + 1], 0),
                 inc=s_tk, inck=16)
        G_TOK = s_tk.n
        p.wait("tensor", s_tk, G_TOK)
        p.wait("tensor", s_v, V_FC1)
        p.wait("tensor", s_v, V_FC2)
        for sub in range(nsub):
            for k in range(8):
                tb = k % 2
                if pbm[tb]:
                    p.wait("tensor", s_v, pbm[tb])
                tt = p.do("tensor", "transpose", pb[tb][:, 0:128],
                          tb4[:, 1024 * sub + 128 * k:1024 * sub + 128 * (k + 1)],
                          cn["eye_b"][:], inc=s_t)
                p.wait("vector", s_t, tt)
                pbm[tb] = p.do("vector", "tensor_copy",
                               out=tf4[:, 512 * k + 128 * sub:512 * k + 128 * (sub + 1)],
                               in_=pb[tb][:, 0:128], inc=s_v)
        at.setdefault("V_TOKT", []).append(s_v.n)
        V_TT = s_v.n
        NT = 128 * nsub
        if _ES < 4:
            continue
        p.wait("tensor", s_v, V_TT)
        for m in range(32):
            bank = pq[m % 2]
            if last_pq[m % 2]:
                p.wait("tensor", s_a, last_pq[m % 2])
            for k in range(8):
                p.do("tensor", "matmul", bank[:, 0:NT],
                     fc1T[:, 4096 * k + 128 * m:4096 * k + 128 * (m + 1)],
                     tf4[:, 512 * k:512 * k + NT], start=(k == 0), stop=(k == 7),
                     inc=s_t if k == 7 else None)
            T_H = s_t.n
            p.wait("scalar", s_t, T_H)
            last_pq[m % 2] = p.do(
                "scalar", "activation", out=hb4[:, 512 * m:512 * m + NT],
                in_=bank[:, 0:NT], func=AF.Gelu, bias=fc1bc[:, m:m + 1], inc=s_a)
        A_HL = s_a.n
        if _ES < 5:
            continue
        p.wait("tensor", s_a, A_HL)
        for tsub in range(nsub):
            yb = ytile[0]
            for ch in range(2):
                bank = pk[ch]
                if last_pk[ch]:
                    p.wait("tensor", s_a, last_pk[ch])
                for m in range(32):
                    p.do("tensor", "matmul", bank[:],
                         hb4[:, 512 * m + 128 * tsub:512 * m + 128 * (tsub + 1)],
                         fc2T[:, 1024 * m + 512 * ch:1024 * m + 512 * (ch + 1)],
                         start=(m == 0), stop=False)
                p.do("tensor", "matmul", bank[:], ones1f[:],
                     biasrow[0:1, 512 * ch:512 * (ch + 1)], start=False, stop=True,
                     inc=s_t if ch == 1 else None)
            T_Y = s_t.n
            if ytm[0]:
                p.wait("scalar", s_y, ytm[0])
            p.wait("scalar", s_t, T_Y)
            p.do("scalar", "activation", out=yb[:, 0:512], in_=pk[0][:], func=AF.Identity)
            A_Y = p.do("scalar", "activation", out=yb[:, 512:1024], in_=pk[1][:],
                       func=AF.Identity, inc=s_a)
            last_pk[0] = A_Y; last_pk[1] = A_Y
            ytm[0] = p.do("scalar", "dma_start", out=ysl[4 * ti + tsub], in_=yb[:],
                          inc=s_y, inck=16)
        if fused and ti in (4, 9, 14, 19):
            stage = (4, 9, 14, 19).index(ti)
            r0 = (0, 2560, 5120, 7680)[stage]
            r1 = (2560, 5120, 7680, 9856)[stage]
            sname = p.sem(f"s_yag{stage}")
            p.wait("gpsimd", s_y, s_y.n)
            yout = io["yall_d"][8 * r0:8 * r1, :]
            p.do("gpsimd", "collective_compute", "AllGather", ALU.bypass,
                 [list(range(NCORE))],
                 [io["ylocal_d"][r0:r1, :].opt()],
                 [yout.opt()], inc=sname)
            at.setdefault("S_YAG", []).append((sname, sname.n))
    p.wait("scalar", s_y, s_y.n)
    p.wait("gpsimd", s_g, s_g.n)
    at["S_Y"] = (s_y, s_y.n)
    return dict()


def build_combine(p, io, at, sbc, fused):
    nc = p.nc
    cn = at["cn"]; coreinfo = at["coreinfo"]
    pq, pk, pt, pb = at["pq"], at["pk"], at["pt"], at["pb"]
    s_v, s_a, s_t, s_g = at["s_v"], at["s_a"], at["s_t"], at["s_g"]
    s_ldc = p.sem("s_ldc")
    s_out = p.sem("s_out")

    weffc = sbc("weffc", [128, 128], F32)
    gsluc = sbc("gsluc", [128, 128], U32)
    y0 = [sbc(f"y0_{i}", [128, 1024], F32) for i in range(2)]
    y1 = [sbc(f"y1_{i}", [128, 1024], F32) for i in range(2)]
    x1c = [sbc(f"x1c{i}", [128, 1024], F32) for i in range(2)]
    xn2c = [sbc(f"xn2c{i}", [128, 1024], F32) for i in range(2)]
    fo = [sbc(f"fo{i}", [128, 1024], F32) for i in range(2)]
    stc = [sbc(f"stc{i}", [128, 8], F32) for i in range(2)]
    sinkc = sbc("sinkc", [128, 1024], F32)
    wsum = sbc("wsum", [128, 128], F32)

    if fused:
        for sname, val in at.get("S_YAG", []):
            p.wait("gpsimd", sname, val)
            p.wait("sync", sname, val)
    srt, rtst = at["RT_ST"]
    p.wait("sync", srt, rtst)
    p.do("sync", "dma_start", out=weffc[:], in_=io["weff_d"], inc=s_ldc, inck=16)
    p.do("sync", "dma_start", out=gsluc[:], in_=io["gsl_d"], inc=s_ldc, inck=16)
    LD_W = s_ldc.n
    p.wait("vector", s_ldc, LD_W)
    V_WS = p.do("vector", "tensor_tensor", out=wsum[:, 0:64], in0=weffc[:, 0:64],
                in1=weffc[:, 64:128], op=ALU.add, inc=s_v)
    p.wait("gpsimd", s_ldc, LD_W)

    x1r = io["x1_d"].rearrange("(r p) n -> r p n", p=128)
    xn2r = io["xn2_d"].rearrange("(r p) n -> r p n", p=128)
    outr = io["out"].rearrange("(r p) n -> r p n", p=128)
    VCB = [0] * 64
    OUTM = [0] * 64
    s_gy = [p.sem("s_gy0"), p.sem("s_gy1")]
    s_lx = [p.sem("s_lx0"), p.sem("s_lx1")]
    for lcol in range(64):
        par = lcol % 2
        Y0, Y1, X1C, XN2C, FO, STC = y0[par], y1[par], x1c[par], xn2c[par], fo[par], stc[par]
        if lcol >= 2:
            p.wait("gpsimd", s_v, VCB[lcol - 2])
            p.wait("sync", s_v, VCB[lcol - 2])
            p.wait("vector", s_out, OUTM[lcol - 2])
        p.do("gpsimd", "indirect_dma_start", out=Y0[:], out_offset=None,
             in_=io["yall_d"], in_offset=IOA(gsluc[:, lcol:lcol + 1], 0),
             inc=s_gy[par], inck=16)
        p.do("gpsimd", "indirect_dma_start", out=Y1[:], out_offset=None,
             in_=io["yall_d"], in_offset=IOA(gsluc[:, 64 + lcol:65 + lcol], 0),
             inc=s_gy[par], inck=16)
        G_Y = s_gy[par].n
        p.do("sync", "dma_start", out=X1C[:], in_=x1r[lcol], inc=s_lx[par], inck=16)
        p.do("sync", "dma_start", out=XN2C[:], in_=xn2r[lcol], inc=s_lx[par], inck=16)
        L_X = s_lx[par].n
        p.wait("vector", s_gy[par], G_Y)
        p.wait("vector", s_lx[par], L_X)
        p.wait("vector", s_v, V_WS)
        p.do("vector", "tensor_scalar_mul", out=FO[:], in0=Y0[:],
             scalar1=weffc[:, lcol:lcol + 1])
        p.do("vector", "scalar_tensor_tensor", out=FO[:], in0=Y1[:],
             scalar=weffc[:, 64 + lcol:65 + lcol], in1=FO[:], op0=ALU.mult, op1=ALU.add)
        V_FO = p.do("vector", "scalar_tensor_tensor", out=FO[:], in0=XN2C[:],
                    scalar=wsum[:, lcol:lcol + 1], in1=FO[:], op0=ALU.mult, op1=ALU.add,
                    inc=s_v)
        p.wait("scalar", s_v, V_FO)
        p.do("scalar", "activation", out=sinkc[:], in_=FO[:], func=AF.Identity,
             accum_out=STC[:, 0:1])
        A_S1 = p.do("scalar", "activation", out=sinkc[:], in_=FO[:], func=AF.Square,
                    accum_out=STC[:, 1:2], inc=s_a)
        p.wait("vector", s_a, A_S1)
        p.do("vector", "tensor_scalar_mul", out=STC[:, 2:3], in0=STC[:, 0:1], scalar1=1.0 / E)
        p.do("vector", "tensor_scalar", out=STC[:, 3:4], in0=STC[:, 2:3],
             scalar1=STC[:, 2:3], scalar2=None, op0=ALU.mult)
        V_V = p.do("vector", "tensor_scalar", out=STC[:, 4:5], in0=STC[:, 1:2],
                   scalar1=1.0 / E, scalar2=STC[:, 3:4], op0=ALU.mult, op1=ALU.subtract,
                   inc=s_v)
        p.wait("scalar", s_v, V_V)
        p.do("scalar", "activation", out=STC[:, 5:6], in_=STC[:, 4:5], func=AF.Ln,
             bias=LN_EPS)
        A_R = p.do("scalar", "activation", out=STC[:, 6:7], in_=STC[:, 5:6], func=AF.Exp,
                   scale=-0.5, inc=s_a)
        p.wait("vector", s_a, A_R)
        p.do("vector", "tensor_scalar", out=FO[:], in0=FO[:], scalar1=STC[:, 2:3],
             scalar2=STC[:, 6:7], op0=ALU.subtract, op1=ALU.mult)
        VCB[lcol] = p.do("vector", "tensor_tensor", out=FO[:], in0=FO[:], in1=X1C[:],
                         op=ALU.add, inc=s_v)
        p.wait("scalar", s_v, VCB[lcol])
        p.do("scalar", "dma_start", out=outr[lcol], in_=FO[:], inc=s_out, inck=16)
        OUTM[lcol] = s_out.n
    p.wait("scalar", s_out, s_out.n)
    return dict()
from concourse.bass_utils import run_bass_kernel_spmd


def _uniq(nc, name):
    if not hasattr(nc, "_mk_names"):
        nc._mk_names = {}
    k = nc._mk_names.get(name, 0)
    nc._mk_names[name] = k + 1
    return name if k == 0 else f"{name}_{k}"


def _alloc(nc, ctx):
    def sbf(name, shp, dt=F32):
        return ctx.enter_context(nc.sbuf_tensor(_uniq(nc, name), shp, dt))
    def psf(name, shp, dt=F32):
        return ctx.enter_context(nc.psum_tensor(_uniq(nc, name), shp, dt))
    return sbf, psf


def common_base(p, io, sb, ps, load_route=False, needed=None, reuse=None):
    nc = p.nc
    names = needed if needed is not None else list(CONST_DTYPES)
    s_v = p.sem("s_v"); s_a = p.sem("s_a"); s_t = p.sem("s_t")
    s_g = p.sem("s_g"); s_st = p.sem("s_st"); s_w = p.sem("s_w")
    if reuse is not None:
        cn, ones1f, coreinfo = reuse
    else:
        cn = {name: sb("c_" + name, list(host_consts()[name].shape), CONST_DTYPES[name])
              for name in names}
        ones1f = sb("ones1f", [1, 128], F32)
        coreinfo = sb("cinfo_s", [128, 8], F32)
        for name in cn:
            p.do("sync", "dma_start", out=cn[name][:], in_=io[name], inc=s_w, inck=16)
        p.do("sync", "dma_start", out=coreinfo[:], in_=io["coreinfo"], inc=s_w, inck=16)
        register_kvals(p, sb, s_w)
        p.do("vector", "memset", ones1f[:], 1.0, inc=s_v)
    at = dict(cn=cn, ones1f=ones1f, coreinfo=coreinfo,
              pq=[ps("pq0", [128, 512], F32), ps("pq1", [128, 512], F32)],
              pk=[ps("pk0", [128, 512], F32), ps("pk1", [128, 512], F32)],
              pt=[ps("pt0", [128, 512], F32), ps("pt1", [128, 512], F32)],
              pb=[ps("pb0", [128, 1024], BF16), ps("pb1", [128, 1024], BF16)],
              s_v=s_v, s_a=s_a, s_t=s_t, s_g=s_g, s_st=s_st,
              ST_END=0, VCH=[0], RT_ST=(s_st, 0))
    if load_route:
        e0sb = sb("e0sb", [128, 64]); e1sb = sb("e1sb", [128, 64])
        w0sb = sb("w0sb", [128, 64]); w1sb = sb("w1sb", [128, 64])
        rr = io["route_d"].rearrange("(a p) n -> a p n", p=128)
        for ai, dst in ((0, e0sb), (1, e1sb), (2, w0sb), (3, w1sb)):
            p.do("sync", "dma_start", out=dst[:], in_=rr[ai], inc=s_w, inck=16)
        at.update(e0sb=e0sb, e1sb=e1sb, w0sb=w0sb, w1sb=w1sb)
    W0 = s_w.n
    for eng in ("vector", "scalar", "tensor", "gpsimd"):
        p.wait(eng, s_w, W0)
    p.wait("tensor", s_v, s_v.n)
    return at


def _consts_io(nc):
    io = {}
    hc = host_consts()
    for name in CONST_DTYPES:
        io[name] = nc.inline_tensor(hc[name], name=name + "_c").ap()
    io["coreinfo"] = nc.dram_tensor("coreinfo", [128, 8], F32, kind="ExternalInput").ap()
    return io


def build_L1():
    nc = bass.Bass("TRN2", target_bir_lowering=False, detect_race_conditions=False)
    io = _consts_io(nc)
    io["x"] = nc.dram_tensor("x", [2, 4096, 1024], F32, kind="ExternalInput").ap()
    for nm, shp in [("wq", [1024, 1024]), ("wk", [1024, 1024]), ("wv", [1024, 1024]),
                    ("wo", [1024, 1024]), ("wf1", [1024, 64]), ("wg1", [1024, 64]),
                    ("wf2", [64, 1024]), ("wg2", [64, 1024]), ("nw", [1024]),
                    ("rw", [8, 1024]), ("rb", [8])]:
        io[nm] = nc.dram_tensor(nm, shp, F32, kind="ExternalInput").ap()
    io["x1_d"] = nc.dram_tensor("x1_d", [8192, 1024], F32, kind="ExternalOutput").ap()
    io["xn2_d"] = nc.dram_tensor("xn2_d", [8192, 1024], F32, kind="ExternalOutput").ap()
    io["xn2b_d"] = nc.dram_tensor("xn2b_d", [8192, 1024], BF16, kind="ExternalOutput").ap()
    io["route_d"] = nc.dram_tensor("route_d", [512, 64], F32, kind="ExternalOutput").ap()
    io["wod_d"] = nc.dram_tensor("wod_d", [1024, 1024], F32, kind="Internal").ap()
    with ExitStack() as ctx:
        sbf, psf = _alloc(nc, ctx)
        p = P(nc, ctx)
        at = build_attn(p, io, sbf, psf)
        p.wait("scalar", at["s_st"], at["s_st"].n)
        p.wait("gpsimd", at["s_g"], at["s_g"].n)
        p.drain_all()
        p.drain_all()
        p.drain_all()
        p.drain_all()
        block = ctx.enter_context(nc.Block())
        p.play(block)
    return nc


def build_L2():
    nc = bass.Bass("TRN2", target_bir_lowering=False, detect_race_conditions=False)
    io = _consts_io(nc)
    io["routeall_d"] = nc.dram_tensor("routeall_d", [8 * 512, 64], F32,
                                      kind="ExternalInput").ap()
    io["route_d"] = nc.dram_tensor("route_d", [512, 64], F32, kind="ExternalInput").ap()
    io["posownT_d"] = nc.dram_tensor("posownT_d", [1024, 128], F32, kind="Internal").ap()
    io["map_d"] = nc.dram_tensor("map_d", [MAPN], F32, kind="ExternalOutput").ap()
    io["weff_d"] = nc.dram_tensor("weff_d", [128, 128], F32, kind="ExternalOutput").ap()
    io["gsl_d"] = nc.dram_tensor("gsl_d", [128, 128], U32, kind="ExternalOutput").ap()
    import os as _os
    if _os.environ.get("MK_DEBUG_L2"):
        io["dbg_csoff"] = nc.dram_tensor("dbg_csoff", [16, 1024], F32, kind="ExternalOutput").ap()
        io["dbg_posown"] = nc.dram_tensor("dbg_posown", [128, 1024], F32, kind="ExternalOutput").ap()
        io["dbg_pos"] = nc.dram_tensor("dbg_pos", [128, 8192], F32, kind="ExternalOutput").ap()
        io["dbg_posloc"] = nc.dram_tensor("dbg_posloc", [128, 128], F32, kind="ExternalOutput").ap()
        io["dbg_postmp"] = nc.dram_tensor("dbg_postmp", [128, 1024], F32, kind="ExternalOutput").ap()
        io["dbg_ownposT"] = nc.dram_tensor("dbg_ownposT", [128, 128], F32, kind="ExternalOutput").ap()
        io["dbg_idx"] = nc.dram_tensor("dbg_idx", [128, 1], U32, kind="ExternalOutput").ap()
    with ExitStack() as ctx:
        sbf, psf = _alloc(nc, ctx)
        p = P(nc, ctx)
        at = common_base(p, io, sbf, psf, load_route=True)
        build_routing(p, io, at, sbf, fused=False)
        if "RT_ST" in at and at["RT_ST"][1] > 0:
            srt, rtst = at["RT_ST"]
            p.wait("scalar", srt, rtst)
        p.wait("gpsimd", at["s_g"], at["s_g"].n)
        p.wait("vector", at["s_v"], at["s_v"].n)
        p.wait("scalar", at["s_a"], at["s_a"].n)
        p.wait("tensor", at["s_t"], at["s_t"].n)
        p.drain_all()
        p.drain_all()
        p.drain_all()
        p.drain_all()
        block = ctx.enter_context(nc.Block())
        p.play(block)
    return nc


def build_L3():
    nc = bass.Bass("TRN2", target_bir_lowering=False, detect_race_conditions=False)
    io = _consts_io(nc)
    io["maps8_d"] = nc.dram_tensor("maps8_d", [8 * MAPN], F32, kind="ExternalInput").ap()
    io["mapsum_d"] = nc.dram_tensor("mapsum_d", [MAPN], F32, kind="Internal").ap()
    io["xnall_d"] = nc.dram_tensor("xnall_d", [65536, 1024], BF16, kind="ExternalInput").ap()
    io["fc1w"] = nc.dram_tensor("fc1w", [4096, 1024], F32, kind="ExternalInput").ap()
    io["fc1b"] = nc.dram_tensor("fc1b", [4096], F32, kind="ExternalInput").ap()
    io["fc2w"] = nc.dram_tensor("fc2w", [1024, 4096], F32, kind="ExternalInput").ap()
    io["fc2b"] = nc.dram_tensor("fc2b", [1024], F32, kind="ExternalInput").ap()
    io["ls"] = nc.dram_tensor("ls", [1024], F32, kind="ExternalInput").ap()
    io["ylocal_d"] = nc.dram_tensor("ylocal_d", [9856, 1024], F32, kind="ExternalOutput").ap()
    import os as _os
    if _os.environ.get("MK_DEBUG_L3"):
        io["dbg_midx"] = nc.dram_tensor("dbg_midx", [128, 1], U32, kind="ExternalOutput").ap()
        io["dbg_ci"] = nc.dram_tensor("dbg_ci", [128, 8], F32, kind="ExternalOutput").ap()
        io["dbg_ci2"] = nc.dram_tensor("dbg_ci2", [128, 8], F32, kind="ExternalOutput").ap()
    with ExitStack() as ctx:
        sbf, psf = _alloc(nc, ctx)
        p = P(nc, ctx)
        at = common_base(p, io, sbf, psf, needed=["eye_f", "eye_b", "iotacol"])
        build_expert(p, io, at, sbf, fused=False)
        p.drain_all()
        p.drain_all()
        p.drain_all()
        p.drain_all()
        block = ctx.enter_context(nc.Block())
        p.play(block)
    return nc


def build_L4():
    nc = bass.Bass("TRN2", target_bir_lowering=False, detect_race_conditions=False)
    io = _consts_io(nc)
    io["yall_d"] = nc.dram_tensor("yall_d", [78848, 1024], F32, kind="ExternalInput").ap()
    io["weff_d"] = nc.dram_tensor("weff_d", [128, 128], F32, kind="ExternalInput").ap()
    io["gsl_d"] = nc.dram_tensor("gsl_d", [128, 128], U32, kind="ExternalInput").ap()
    io["x1_d"] = nc.dram_tensor("x1_d", [8192, 1024], F32, kind="ExternalInput").ap()
    io["xn2_d"] = nc.dram_tensor("xn2_d", [8192, 1024], F32, kind="ExternalInput").ap()
    io["out"] = nc.dram_tensor("out", [8192, 1024], F32, kind="ExternalOutput").ap()
    with ExitStack() as ctx:
        sbf, psf = _alloc(nc, ctx)
        p = P(nc, ctx)
        at = common_base(p, io, sbf, psf, needed=["eye_f", "eye_b", "iotacol"])
        build_combine(p, io, at, sbf, fused=False)
        p.drain_all()
        p.drain_all()
        p.drain_all()
        p.drain_all()
        block = ctx.enter_context(nc.Block())
        p.play(block)
    return nc


def build_fused():
    nc = bass.Bass("TRN2", target_bir_lowering=False, detect_race_conditions=False)
    io = _consts_io(nc)
    io["x"] = nc.dram_tensor("x", [2, 4096, 1024], F32, kind="ExternalInput").ap()
    for nm, shp in [("wq", [1024, 1024]), ("wk", [1024, 1024]), ("wv", [1024, 1024]),
                    ("wo", [1024, 1024]), ("wf1", [1024, 64]), ("wg1", [1024, 64]),
                    ("wf2", [64, 1024]), ("wg2", [64, 1024]), ("nw", [1024]),
                    ("rw", [8, 1024]), ("rb", [8])]:
        io[nm] = nc.dram_tensor(nm, shp, F32, kind="ExternalInput").ap()
    io["fc1w"] = nc.dram_tensor("fc1w", [4096, 1024], F32, kind="ExternalInput").ap()
    io["fc1b"] = nc.dram_tensor("fc1b", [4096], F32, kind="ExternalInput").ap()
    io["fc2w"] = nc.dram_tensor("fc2w", [1024, 4096], F32, kind="ExternalInput").ap()
    io["fc2b"] = nc.dram_tensor("fc2b", [1024], F32, kind="ExternalInput").ap()
    io["ls"] = nc.dram_tensor("ls", [1024], F32, kind="ExternalInput").ap()
    io["out"] = nc.dram_tensor("out", [8192, 1024], F32, kind="ExternalOutput").ap()
    io["wod_d"] = nc.dram_tensor("wod_d", [1024, 1024], F32, kind="Internal").ap()
    io["x1_d"] = nc.dram_tensor("x1_d", [8192, 1024], F32, kind="Internal").ap()
    io["xn2_d"] = nc.dram_tensor("xn2_d", [8192, 1024], F32, kind="Internal").ap()
    io["xn2b_d"] = nc.dram_tensor("xn2b_d", [8192, 1024], BF16, kind="Internal").ap()
    io["xnall_d"] = nc.dram_tensor("xnall_d", [65536, 1024], BF16, kind="Internal").ap()
    io["route_d"] = nc.dram_tensor("route_d", [512, 64], F32, kind="Internal").ap()
    io["routeall_d"] = nc.dram_tensor("routeall_d", [8 * 512, 64], F32, kind="Internal").ap()
    io["posownT_d"] = nc.dram_tensor("posownT_d", [1024, 128], F32, kind="Internal").ap()
    io["map_d"] = nc.dram_tensor("map_d", [MAPN], F32, kind="Internal").ap()
    io["maps8_d"] = nc.dram_tensor("maps8_d", [8 * MAPN], F32, kind="Internal").ap()
    io["mapsum_d"] = nc.dram_tensor("mapsum_d", [MAPN], F32, kind="Internal").ap()
    io["weff_d"] = nc.dram_tensor("weff_d", [128, 128], F32, kind="Internal").ap()
    io["gsl_d"] = nc.dram_tensor("gsl_d", [128, 128], U32, kind="Internal").ap()
    io["ylocal_d"] = nc.dram_tensor("ylocal_d", [9856, 1024], F32, kind="Internal").ap()
    io["yall_d"] = nc.dram_tensor("yall_d", [78848, 1024], F32, kind="Internal").ap()

    PERSIST = set("c_" + n for n in CONST_DTYPES) | {"ones1f", "onescol", "cinfo_s",
                                                     "konst"}
    with ExitStack() as ctx:
        p = P(nc, ctx)
        persisted = {}
        with ExitStack() as actx:
            def sbA(name, shp, dt=F32):
                tgt = ctx if name in PERSIST else actx
                t = tgt.enter_context(nc.sbuf_tensor(_uniq(nc, name), shp, dt))
                if name in PERSIST:
                    persisted[name] = t
                return t
            def psA(name, shp, dt=F32):
                return actx.enter_context(nc.psum_tensor(_uniq(nc, name), shp, dt))
            at1 = build_attn(p, io, sbA, psA)
            sv1, sa1, st1, sg1, sst1 = (at1["s_v"], at1["s_a"], at1["s_t"],
                                        at1["s_g"], at1["s_st"])
            for s in (sa1, st1, sg1, sst1):
                p.wait("vector", s, s.n)
            BAR1 = p.do("vector", "engine_nop", inc=sv1)
            for eng in ("scalar", "tensor", "gpsimd", "sync"):
                p.wait(eng, sv1, BAR1)
        cn1 = {name: persisted["c_" + name] for name in CONST_DTYPES}
        sbf, psf = _alloc(nc, ctx)
        at = common_base(p, io, sbf, psf, load_route=True,
                         reuse=(cn1, persisted["ones1f"], persisted["cinfo_s"]))
        s_cx = p.sem("s_cx")
        p.do("gpsimd", "collective_compute", "AllGather", ALU.bypass,
             [list(range(NCORE))], [io["xn2b_d"].opt()], [io["xnall_d"].opt()],
             inc=s_cx)
        at["S_CX"] = (s_cx, s_cx.n)
        s_v, s_a, s_t, s_g = at["s_v"], at["s_a"], at["s_t"], at["s_g"]
        with ExitStack() as rctx:
            def sbR(name, shp, dt=F32):
                return rctx.enter_context(nc.sbuf_tensor(_uniq(nc, name), shp, dt))
            build_routing(p, io, at, sbR, fused=True)
            srt, rtst = at["RT_ST"]
            p.wait("vector", srt, rtst)
            for s in (s_a, s_t, s_g):
                p.wait("vector", s, s.n)
            BAR2 = p.do("vector", "engine_nop", inc=s_v)
            for eng in ("scalar", "tensor", "gpsimd", "sync"):
                p.wait(eng, s_v, BAR2)
        with ExitStack() as ectx:
            def sbE(name, shp, dt=F32):
                return ectx.enter_context(nc.sbuf_tensor(_uniq(nc, name), shp, dt))
            build_expert(p, io, at, sbE, fused=True)
            sy, syn = at["S_Y"]
            p.wait("vector", sy, syn)
            for s in (s_a, s_t, s_g):
                p.wait("vector", s, s.n)
            BAR3 = p.do("vector", "engine_nop", inc=s_v)
            for eng in ("scalar", "tensor", "gpsimd", "sync"):
                p.wait(eng, s_v, BAR3)
        build_combine(p, io, at, sbf, fused=True)
        p.drain_all()
        p.drain_all()
        p.drain_all()
        p.drain_all()
        block = ctx.enter_context(nc.Block())
        p.play(block)
    return nc


def _percore_const_inputs():
    return {}


def kernel(**inputs):
    x = np.ascontiguousarray(inputs["L0_attn_x"])
    consts = _percore_const_inputs()
    core_inf = []
    for c in range(8):
        ci = np.zeros((128, 8), np.float32)
        ci[:, 0] = c; ci[:, 1] = 8192 * c; ci[:, 2] = 64 * c; ci[:, 3] = 77 * c
        core_inf.append(ci)

    shared_w = dict(
        wq=inputs["L0_attn_Wq"], wk=inputs["L0_attn_Wk"], wv=inputs["L0_attn_Wv"],
        wo=inputs["L0_attn_Wo"], wf1=inputs["L0_attn_Wf1"], wf2=inputs["L0_attn_Wf2"],
        wg1=inputs["L0_attn_Wg1"], wg2=inputs["L0_attn_Wg2"],
        nw=inputs["L0_attn_norm_weight"], rw=inputs["L0_ffn_router_w"],
        rb=inputs["L0_ffn_router_b"])

    ncF = build_fused()
    maps = []
    for c in range(8):
        m = {**consts, "coreinfo": core_inf[c], "x": x[2 * c:2 * c + 2],
             **{k: np.ascontiguousarray(v) for k, v in shared_w.items()},
             "fc1w": np.ascontiguousarray(inputs["L0_ffn_fc1_w"][c]),
             "fc1b": np.ascontiguousarray(inputs["L0_ffn_fc1_b"][c]),
             "fc2w": np.ascontiguousarray(inputs["L0_ffn_fc2_w"][c]),
             "fc2b": np.ascontiguousarray(inputs["L0_ffn_fc2_b"][c]),
             "ls": np.ascontiguousarray(inputs["L0_ffn_layer_scale"][c])}
        maps.append(m)
    r = run_bass_kernel_spmd(ncF, maps, core_ids=list(range(8))).results
    out = np.concatenate([r[c]["out"].reshape(2, 4096, 1024) for c in range(8)], axis=0)
    return out
